# version 01
# baseline (speedup 1.0000x reference)
import contextlib
from collections import defaultdict

import numpy as np
import concourse.bass as bass
import concourse.mybir as mybir
from concourse.bass_utils import run_bass_kernel_spmd

F32 = mybir.dt.float32
BF16 = mybir.dt.bfloat16
AF = mybir.ActivationFunctionType
ALU = mybir.AluOpType
AX = mybir.AxisListType

NCORES = 8
B = 2
T = 8192
TC = 2048
D = 1024
FD = 513
FP = 640
DI = 2048


class Plan:
    ENG = ("sync", "gpsimd", "tensor", "vector", "scalar")

    NSLOT = 12

    def __init__(self):
        self.steps = []
        self.cnt = defaultdict(int)
        self._rr = defaultdict(int)

    def _add(self, engine, fn, waits, sem):
        inc = 16 if sem.startswith("q") else 1
        self.cnt[sem] += inc
        tok = (sem, self.cnt[sem])
        self.steps.append((engine, fn, waits, (sem, inc)))
        return tok

    def dma(self, q, out, in_, waits=(), **kw):
        slot = self._rr[q] % self.NSLOT
        self._rr[q] += 1
        sem = f"q{q[0]}{slot}"
        prev = (sem, self.cnt[sem]) if self.cnt[sem] else None

        def fn(eng):
            return eng.dma_start(out=out, in_=in_, **kw)
        return self._add(q, fn, W(waits, prev), sem)

    def pe(self, fn, waits=()):
        return self._add("tensor", fn, waits, "cp")

    def dve(self, fn, waits=()):
        return self._add("vector", fn, waits, "cv")

    def act(self, fn, waits=()):
        return self._add("scalar", fn, waits, "ca")

    def emit(self, block, sems):
        steps = self.steps
        eng_attr = {"sync": block.sync, "gpsimd": block.gpsimd,
                    "tensor": block.tensor, "vector": block.vector,
                    "scalar": block.scalar}
        for engine in self.ENG:
            if not any(s[0] == engine for s in steps):
                continue

            def body(eng, _engine=engine):
                waited = defaultdict(int)
                for (e, fn, waits, (sem, inc)) in steps:
                    if e != _engine:
                        continue
                    for (ws, wv) in waits:
                        if waited[ws] < wv:
                            eng.wait_ge(sems[ws], wv)
                            waited[ws] = wv
                    fn(eng).then_inc(sems[sem], inc)

            eng_attr[engine](body)

    def sem_names(self):
        return sorted(set(s for (_, _, _, (s, _)) in self.steps))


def W(*toks):
    best = {}
    for t in toks:
        if t is None:
            continue
        if isinstance(t, (list, tuple)) and len(t) == 0:
            continue
        if isinstance(t, (list, tuple)) and t and isinstance(t[0], (list, tuple)):
            for s, v in t:
                if best.get(s, -1) < v:
                    best[s] = v
        elif isinstance(t, tuple) and len(t) == 2 and isinstance(t[0], str):
            s, v = t
            if best.get(s, -1) < v:
                best[s] = v
        else:
            raise ValueError(f"bad token {t}")
    return tuple(best.items())


def run_plan(nc, pl, in_map_builder=None):
    with contextlib.ExitStack() as st:
        sems = {}
        for name in pl.sem_names():
            sems[name] = st.enter_context(nc.semaphore(name))
        block = st.enter_context(nc.Block())
        pl.emit(block, sems)
    return nc


def pm(vec):
    v = np.ascontiguousarray(np.asarray(vec, np.float32))
    n = v.shape[0]
    return np.ascontiguousarray(v.reshape(n // 128, 128).T)


_CONST_CACHE = {}


def dft_mats():
    if "dft" not in _CONST_CACHE:
        f = np.arange(FD)
        d = np.arange(D)
        ang = 2 * np.pi * np.outer(d, f) / D
        Fre = np.cos(ang).astype(np.float32)
        Fim = (-np.sin(ang)).astype(np.float32)
        w = np.full(FD, 2.0, np.float32); w[0] = 1.0; w[FD - 1] = 1.0
        ang2 = 2 * np.pi * np.outer(f, d) / D
        Ire = (w[:, None] * np.cos(ang2) / D).astype(np.float32)
        Iim = (-w[:, None] * np.sin(ang2) / D).astype(np.float32)
        Fwd = np.zeros((D, 2 * FP), np.float32)
        Fwd[:, :FD] = Fre
        Fwd[:, FP:FP + FD] = Fim
        Inv = np.zeros((2 * FP, D), np.float32)
        Inv[:FD] = Ire
        Inv[FP:FP + FD] = Iim
        _CONST_CACHE["dft"] = (np.ascontiguousarray(Fwd), np.ascontiguousarray(Inv))
    return _CONST_CACHE["dft"]


def tril_mask():
    if "tril" not in _CONST_CACHE:
        _CONST_CACHE["tril"] = np.ascontiguousarray(
            np.tril(np.ones((128, 128), np.float32)).T)
    return _CONST_CACHE["tril"]


def eye128():
    if "eye" not in _CONST_CACHE:
        _CONST_CACHE["eye"] = np.eye(128, dtype=np.float32)
    return _CONST_CACHE["eye"]


def emit_ln(pl, x3, xh, sc, ps, dep, gb=None, out_f32=None):
    xc, sqa, sqb = sc["xc"], sc["sqa"], sc["sqb"]
    mrow, rrow, msq, sub = sc["mrow"], sc["rrow"], sc["msq"], sc["sub"]
    ones_col, ones_row = sc["ones_col"], sc["ones_row"]
    xc3 = xc[:].rearrange("p (k t) -> p k t", k=8)
    xh3 = xh[:].rearrange("p (k t) -> p k t", k=8)
    NCH = 512
    toks = []
    xc_readers = [dep]
    s1_free = None
    rep_free = []
    sq_free = {0: None, 1: None}
    for ch in range(TC // NCH):
        t0 = ch * NCH
        ld = pl.dma("sync", xc3[:, :, :], x3[:, :, t0:t0 + NCH],
                    waits=W(*xc_readers))
        xc_readers = []
        s1 = ps[0:1, 0:NCH]
        s2 = ps[0:1, 512:512 + NCH]
        last1 = last2 = None
        for k in range(8):
            t1 = pl.pe(lambda eng, k=k, s1=s1: eng.matmul(
                s1, ones_col[:], xc3[:, k, :], start=(k == 0), stop=(k == 7)),
                waits=W(ld, s1_free) if k == 0 else ())
            sq = (sqa, sqb)[k % 2]
            sq_t = pl.dve(lambda eng, k=k, sq=sq: eng.tensor_mul(
                sq[:], xc3[:, k, :], xc3[:, k, :]),
                waits=W(ld, sq_free[k % 2]))
            t2 = pl.pe(lambda eng, k=k, s2=s2, sq=sq: eng.matmul(
                s2, ones_col[:], sq[:], start=(k == 0), stop=(k == 7)),
                waits=W(sq_t, s1_free if k == 0 else None))
            sq_free[k % 2] = t2
            xc_readers.append(sq_t)
            last1, last2 = t1, t2
        d1 = pl.dve(lambda eng, t0=t0: eng.tensor_scalar_mul(
            mrow[:, t0:t0 + NCH], s1, 1.0 / D), waits=W(last1, last2))
        d2 = pl.dve(lambda eng, t0=t0: eng.tensor_scalar_mul(
            rrow[:, t0:t0 + NCH], s2, 1.0 / D), waits=W(last2,))
        s1_free = W(d1, d2)[0]
        d3 = pl.dve(lambda eng, t0=t0: eng.tensor_mul(
            msq[:], mrow[0:1, t0:t0 + NCH], mrow[0:1, t0:t0 + NCH]), waits=W(d1,))
        d4 = pl.dve(lambda eng, t0=t0: eng.tensor_sub(
            rrow[:, t0:t0 + NCH], rrow[:, t0:t0 + NCH], msq[:]), waits=W(d2, d3))
        a1 = pl.act(lambda eng, t0=t0: eng.activation(
            rrow[:, t0:t0 + NCH], rrow[:, t0:t0 + NCH], AF.Ln,
            bias=sc["eps"]), waits=W(d4, sc.get("eps_tok")))
        a2 = pl.act(lambda eng, t0=t0: eng.activation(
            rrow[:, t0:t0 + NCH], rrow[:, t0:t0 + NCH], AF.Exp, scale=-0.5), waits=W(a1,))
        mrep = ps[:, 1024:1024 + NCH]
        rrep = ps[:, 1536:1536 + NCH]
        b1 = pl.pe(lambda eng, t0=t0, mrep=mrep: eng.matmul(
            mrep, ones_row[:], mrow[0:1, t0:t0 + NCH], start=True, stop=True),
            waits=W(d1, d3, *rep_free))
        b2 = pl.pe(lambda eng, t0=t0, rrep=rrep: eng.matmul(
            rrep, ones_row[:], rrow[0:1, t0:t0 + NCH], start=True, stop=True),
            waits=W(a2, *rep_free))
        rep_free = []
        sub_free = None
        for k in range(8):
            s_ = pl.dve(lambda eng, k=k, mrep=mrep: eng.tensor_sub(
                sub[:], xc3[:, k, :], mrep), waits=W(b1, ld, sub_free))
            xc_readers.append(s_)
            if gb is None:
                tgt3 = (out_f32 if out_f32 is not None else xh)[:].rearrange(
                    "p (k t) -> p k t", k=8)
                m_ = pl.dve(lambda eng, k=k, t0=t0, rrep=rrep, tgt3=tgt3: eng.tensor_mul(
                    tgt3[:, k, t0:t0 + NCH], sub[:], rrep), waits=W(s_, b2))
                if out_f32 is not None:
                    m_ = pl.dve(lambda eng, k=k, t0=t0: eng.tensor_copy(
                        xh3[:, k, t0:t0 + NCH],
                        out_f32[:].rearrange("p (k t) -> p k t", k=8)[:, k, t0:t0 + NCH]),
                        waits=W(m_,))
            else:
                g_col, b_col = gb
                m0 = pl.dve(lambda eng, k=k, rrep=rrep: eng.tensor_mul(
                    sub[:], sub[:], rrep), waits=W(s_, b2))
                m_ = pl.dve(lambda eng, k=k, t0=t0: eng.tensor_scalar(
                    xh3[:, k, t0:t0 + NCH], sub[:], g_col[:, k:k + 1], b_col[:, k:k + 1],
                    ALU.mult, ALU.add), waits=W(m0,))
            sub_free = m_
            rep_free.append(m_)
            toks.append(m_)
    return toks, W(*toks)


def alloc_ln_scratch(st, nc, tag="", big_st=None):
    bst = big_st if big_st is not None else st
    sc = {
        "mrow": st.enter_context(nc.sbuf_tensor("ln_mrow" + tag, [1, TC], F32)),
        "rrow": st.enter_context(nc.sbuf_tensor("ln_rrow" + tag, [1, TC], F32)),
        "msq": st.enter_context(nc.sbuf_tensor("ln_msq" + tag, [1, 512], F32)),
        "ones_col": st.enter_context(nc.sbuf_tensor("ones_col" + tag, [128, 1], F32)),
        "ones_row": st.enter_context(nc.sbuf_tensor("ones_row" + tag, [1, 128], F32)),
        "eps_t": st.enter_context(nc.sbuf_tensor("ln_eps" + tag, [1, 1], F32)),
        "ones_bf": st.enter_context(nc.sbuf_tensor("ones_bf" + tag, [128, 1], BF16)),
        "xc": bst.enter_context(nc.sbuf_tensor("ln_xc" + tag, [128, 8 * 512], F32)),
        "sqa": bst.enter_context(nc.sbuf_tensor("ln_sqa" + tag, [128, 512], F32)),
        "sqb": bst.enter_context(nc.sbuf_tensor("ln_sqb" + tag, [128, 512], F32)),
        "sub": bst.enter_context(nc.sbuf_tensor("ln_sub" + tag, [128, 512], F32)),
    }
    sc["eps"] = sc["eps_t"][:]
    return sc


def emit_ln_setup(pl, sc, eps=1e-5):
    t1 = pl.dve(lambda eng: eng.memset(sc["ones_col"][:], 1.0))
    t2 = pl.dve(lambda eng: eng.memset(sc["ones_row"][:], 1.0))
    t3 = pl.dve(lambda eng: eng.memset(sc["eps_t"][:], eps))
    t4 = pl.dve(lambda eng: eng.memset(sc["ones_bf"][:], 1.0))
    sc["eps_tok"] = t3
    return W(t1, t2, t3, t4)


def emit_proj(pl, xh3, w3, kt, mt, psm, cb, dep, bank_free=None, nch=512):
    if bank_free is None:
        bank_free = {}
    toks = []
    nchk = TC // nch
    for m in range(mt):
        for ch in range(nchk):
            bank = (m * nchk + ch) % 4
            pslice = psm[:, bank * nch:(bank + 1) * nch]

            def fgrp(eng, m=m, ch=ch, pslice=pslice):
                ins = None
                for k in range(kt):
                    ins = eng.matmul(pslice, w3[:, k, m, :],
                                     xh3[:, k, ch * nch:(ch + 1) * nch],
                                     start=(k == 0), stop=(k == kt - 1))
                return ins

            mm = pl.pe(fgrp, waits=W(*dep, bank_free.get(bank)))
            tok = cb(m, ch, pslice, mm)
            bank_free[bank] = tok
            toks.append(tok)
    return toks


def build_attn_a(upto=None):
    nc = bass.Bass("TRN2", target_bir_lowering=False, debug=False)
    x = nc.dram_tensor("x", [D, TC], F32, kind="ExternalInput").ap()
    wq = nc.dram_tensor("wq", [D, D], F32, kind="ExternalInput").ap()
    bq = nc.dram_tensor("bq", [128, 8], F32, kind="ExternalInput").ap()
    wd = nc.dram_tensor("wd", [D, 256], F32, kind="ExternalInput").ap()
    bd = nc.dram_tensor("bd", [128, 2], F32, kind="ExternalInput").ap()
    wu = nc.dram_tensor("wu", [256, 2048], F32, kind="ExternalInput").ap()
    bu = nc.dram_tensor("bu", [128, 16], F32, kind="ExternalInput").ap()
    wo = nc.dram_tensor("wo", [D, D], F32, kind="ExternalInput").ap()
    bo = nc.dram_tensor("bo", [128, 8], F32, kind="ExternalInput").ap()
    eye = nc.dram_tensor("eye", [128, 128], F32, kind="ExternalInput").ap()
    xo = nc.dram_tensor("xo", [D, TC], F32, kind="ExternalOutput").ap()
    qd = nc.dram_tensor("qd", [D, TC], BF16, kind="Internal").ap()
    kd = nc.dram_tensor("kd", [D, TC], BF16, kind="Internal").ap()
    vd = nc.dram_tensor("vd", [D, TC], BF16, kind="Internal").ap()
    ad = nc.dram_tensor("ad", [TC, D], BF16, kind="Internal").ap()

    pl = Plan()
    x3 = x.rearrange("(k p) t -> p k t", p=128)
    xo3 = xo.rearrange("(k p) t -> p k t", p=128)

    with contextlib.ExitStack() as st:
        wo_sb = st.enter_context(nc.sbuf_tensor("wo_sb", [128, 8 * 1024], BF16))
        bo_sb = st.enter_context(nc.sbuf_tensor("bo_sb", [128, 8], F32))
        eye_sb = st.enter_context(nc.sbuf_tensor("eye_sb", [128, 128], BF16))
        ldwo = pl.dma("gpsimd", wo_sb[:].rearrange("p (k m) -> p k m", k=8),
                      wo.rearrange("(k p) m -> p k m", p=128))
        ldbo = pl.dma("sync", bo_sb[:], bo)
        ldeye = pl.dma("gpsimd", eye_sb[:], eye)

        phase1_done = []
        with contextlib.ExitStack() as st1:
            wq_sb = st1.enter_context(nc.sbuf_tensor("wq_sb", [128, 8 * 1024], BF16))
            bq_sb = st1.enter_context(nc.sbuf_tensor("bq_sb", [128, 8], F32))
            wd_sb = st1.enter_context(nc.sbuf_tensor("wd_sb", [128, 8 * 256], BF16))
            bd_sb = st1.enter_context(nc.sbuf_tensor("bd_sb", [128, 2], F32))
            wu_sb = st1.enter_context(nc.sbuf_tensor("wu_sb", [128, 2 * 2048], BF16))
            bu_sb = st1.enter_context(nc.sbuf_tensor("bu_sb", [128, 16], F32))
            xh = st1.enter_context(nc.sbuf_tensor("xh", [128, 8 * TC], BF16))
            lat = st1.enter_context(nc.sbuf_tensor("lat", [128, 2 * TC], BF16))
            stg = [st1.enter_context(nc.sbuf_tensor(f"stg{i}", [128, 512], BF16))
                   for i in range(4)]
            sc = alloc_ln_scratch(st1, nc)
            ps_ln = st1.enter_context(nc.psum_tensor("ps_ln", [128, 2048], F32))
            ps_mm = st1.enter_context(nc.psum_tensor("ps_mm", [128, 2048], F32))

            ldw1 = pl.dma("gpsimd", wq_sb[:].rearrange("p (k m) -> p k m", k=8),
                          wq.rearrange("(k p) m -> p k m", p=128))
            ldw2 = pl.dma("gpsimd", wd_sb[:].rearrange("p (k m) -> p k m", k=8),
                          wd.rearrange("(k p) m -> p k m", p=128))
            ldw3 = pl.dma("gpsimd", wu_sb[:].rearrange("p (k m) -> p k m", k=2),
                          wu.rearrange("(k p) m -> p k m", p=128))
            ldb = [pl.dma("sync", bq_sb[:], bq), pl.dma("sync", bd_sb[:], bd),
                   pl.dma("sync", bu_sb[:], bu)]
            cdep = emit_ln_setup(pl, sc)

            ln_toks, ln_all = emit_ln(pl, x3, xh, sc, ps_ln, cdep)
            xh3 = xh[:].rearrange("p (k t) -> p k t", k=8)

            qd3 = qd.rearrange("(m p) t -> p m t", p=128)
            stg_free = {}
            bank_free = {}

            def mk_cb(dst3, bias_sb, nm, scale=1.0):
                def cb(m, ch, ps_ap, mm):
                    sti = (m * 4 + ch) % len(stg)
                    a = pl.act(lambda eng, m=m, sti=sti, ps_ap=ps_ap: eng.activation(
                        stg[sti][:], ps_ap, AF.Identity, bias=bias_sb[:, m:m + 1],
                        scale=scale), waits=W(mm, stg_free.get(sti), *ldb, bias_tok.get(nm)))
                    dtok = pl.dma("sync", dst3[:, m, ch * 512:(ch + 1) * 512],
                                  stg[sti][:], waits=W(a,))
                    stg_free[sti] = dtok
                    return W(a,)[0]
                return cb

            bias_tok = {}
            bias_tok["q"] = pl.dve(lambda eng: eng.tensor_scalar_mul(
                bq_sb[:], bq_sb[:], 0.125), waits=W(*ldb))
            w3q = wq_sb[:].rearrange("p (k m c) -> p k m c", k=8, m=8)
            q_toks = emit_proj(pl, xh3, w3q, 8, 8, ps_mm, mk_cb(qd3, bq_sb, "q", scale=0.125),
                               dep=[ln_all, ldw1], bank_free=bank_free)

            lat3 = lat[:].rearrange("p (k t) -> p k t", k=2)
            w3d = wd_sb[:].rearrange("p (k m c) -> p k m c", k=8, m=2)

            def cb_lat(m, ch, ps_ap, mm):
                a = pl.act(lambda eng, m=m, ch=ch, ps_ap=ps_ap: eng.activation(
                    lat3[:, m, ch * 512:(ch + 1) * 512], ps_ap, AF.Identity,
                    bias=bd_sb[:, m:m + 1]), waits=W(mm, *ldb))
                return a

            lat_toks = emit_proj(pl, xh3, w3d, 8, 2, ps_mm, cb_lat,
                                 dep=[ln_all, ldw2], bank_free=bank_free)

            kd4 = kd.rearrange("(g dh) t -> g dh t", dh=64)
            vd4w = vd.rearrange("(g dh) t -> g dh t", dh=64)
            w3u = wu_sb[:].rearrange("p (k m c) -> p k m c", k=2, m=16)

            def cb_kv(m, ch, ps_ap, mm):
                sti = (m * 4 + ch) % len(stg)
                a = pl.act(lambda eng, m=m, sti=sti, ps_ap=ps_ap: eng.activation(
                    stg[sti][:], ps_ap, AF.Identity, bias=bu_sb[:, m:m + 1]),
                    waits=W(mm, stg_free.get(sti), *ldb))
                d1 = pl.dma("sync", kd4[m, :, ch * 512:(ch + 1) * 512],
                            stg[sti][0:64, :], waits=W(a,))
                d2 = pl.dma("sync", vd4w[m, :, ch * 512:(ch + 1) * 512],
                            stg[sti][64:128, :], waits=W(a,))
                stg_free[sti] = W(d1, d2)
                return W(a,)[0]

            kv_toks = emit_proj(pl, lat3, w3u, 2, 16, ps_mm, cb_kv,
                                dep=[W(*lat_toks), ldw3], bank_free=bank_free)
            phase1_done = [W(*(q_toks + kv_toks + [t for v in stg_free.values() for t in (v if isinstance(v, tuple) and v and isinstance(v[0], tuple) else [v])]))]
            pl.markers = getattr(pl, "markers", {})
            pl.markers["p1"] = (len(pl.steps), phase1_done[0])

        with contextlib.ExitStack() as st2:
            qs = st2.enter_context(nc.sbuf_tensor("qs", [128, 16 * 1024], BF16))
            ks = st2.enter_context(nc.sbuf_tensor("ks", [128, 16 * 1024], BF16))
            vs = st2.enter_context(nc.sbuf_tensor("vs", [128, 64 * 512], BF16))
            wide = st2.enter_context(nc.sbuf_tensor("wide", [128, 4096], F32))
            wide2 = st2.enter_context(nc.sbuf_tensor("wide2", [128, 4096], F32))
            mx = st2.enter_context(nc.sbuf_tensor("mx", [128, 256], F32))
            den = st2.enter_context(nc.sbuf_tensor("den", [128, 256], F32))
            wbf = st2.enter_context(nc.sbuf_tensor("wbf", [128, 4096], BF16))
            wT = st2.enter_context(nc.sbuf_tensor("wT", [128, 4096], BF16))
            avs = [st2.enter_context(nc.sbuf_tensor(f"avs{i}", [128, 512], BF16))
                   for i in range(2)]
            ps_big = st2.enter_context(nc.psum_tensor("ps_big", [128, 4096], F32))

            p1d = phase1_done[0]
            qd4 = qd.rearrange("(h dh) (s u) -> s dh h u", dh=64, s=2)
            kd4 = kd.rearrange("(h dh) (s u) -> s dh h u", dh=64, s=2)
            ldq = W(*[pl.dma("sync",
                             qs[64 * s_:64 * s_ + 64].rearrange("p (h u) -> p h u", h=16),
                             qd4[s_], waits=W(p1d)) for s_ in range(2)])
            ldk = W(*[pl.dma("sync",
                             ks[64 * s_:64 * s_ + 64].rearrange("p (h u) -> p h u", h=16),
                             kd4[s_], waits=W(p1d)) for s_ in range(2)])
            vd4 = vd.rearrange("(g dp) (r v) -> r g dp v", dp=64, r=4)
            ldv = []
            for r in range(4):
                ldv.append(pl.dma("sync",
                                  vs[32 * r:32 * r + 16].rearrange("p (dp v) -> p dp v", dp=64),
                                  vd4[r], waits=W(p1d)))
            pm0 = pl.dve(lambda eng: eng.memset(ps_big[:], 0.0), waits=W(p1d))

            qs3 = qs[:].rearrange("p (h u) -> p h u", h=16)
            ks3 = ks[:].rearrange("p (h u) -> p h u", h=16)
            vs3 = vs[:].rearrange("p (dp v) -> p dp v", dp=64)
            wide4 = wide[:].rearrange("p (b c h2 g) -> p (b c h2) g", b=8, c=16, h2=2)
            wide24 = wide2[:].rearrange("p (b c h2 g) -> p (b c h2) g", b=8, c=16, h2=2)

            av_out_toks = []
            qk_bank_free = {}
            av_bank_free = {}
            avs_free = {}
            for e in range(4):
                bank_toks = []
                for beta in range(8):
                    def fqk(eng, e=e, beta=beta):
                        ins = None
                        for r in range(4):
                            for c in range(16):
                                t = 512 * r + 128 * e + 16 * beta + c
                                s_, u = t // 1024, t % 1024
                                ins = eng.matmul(
                                    ps_big[32 * r:32 * r + 16,
                                           512 * beta + 32 * c:512 * beta + 32 * c + 16],
                                    qs3[64 * s_:64 * s_ + 64, :, u],
                                    ks3[64 * s_:64 * s_ + 64, :, u],
                                    start=True, stop=True, tile_position=(64 * s_, 32 * r),
                                    skip_group_check=True)
                        return ins
                    bt = pl.pe(fqk, waits=W(ldq, ldk, pm0, qk_bank_free.get(beta)))
                    bank_toks.append(bt)
                ev = []
                for beta in range(8):
                    ev.append(pl.dve(lambda eng, beta=beta: eng.tensor_copy(
                        wide[:, 512 * beta:512 * (beta + 1)],
                        ps_big[:, 512 * beta:512 * (beta + 1)]),
                        waits=W(bank_toks[beta],)))
                evw = W(*ev)
                t_mx = pl.dve(lambda eng: eng.tensor_reduce(
                    mx[:], wide4, AX.X, ALU.max), waits=W(evw))
                t_sub = pl.dve(lambda eng: eng.tensor_tensor(
                    wide24, wide4,
                    mx[:].unsqueeze(2).broadcast_to([128, 256, 16]), ALU.subtract),
                    waits=W(t_mx,))
                t_exp = pl.act(lambda eng: eng.activation(
                    wide2[:], wide2[:], AF.Exp), waits=W(t_sub,))
                t_den = pl.dve(lambda eng: eng.tensor_reduce(
                    den[:], wide24, AX.X, ALU.add), waits=W(t_exp,))
                t_rec = pl.dve(lambda eng: eng.reciprocal(den[:], den[:]),
                               waits=W(t_den,))
                t_nrm = pl.dve(lambda eng: eng.tensor_tensor(
                    wbf[:].rearrange("p (x g) -> p x g", g=16), wide24,
                    den[:].unsqueeze(2).broadcast_to([128, 256, 16]), ALU.mult),
                    waits=W(t_rec,))
                for beta in range(8):
                    qk_bank_free[beta] = t_nrm
                tr = []
                for beta in range(8):
                    tr.append(pl.dve(lambda eng, beta=beta: eng.transpose(
                        wT[:, 512 * beta:512 * (beta + 1)],
                        wbf[:, 512 * beta:512 * (beta + 1)]), waits=W(t_nrm,)))
                for j in range(16):
                    beta_of_j = j // 2
                    bank = j % 8

                    def fav(eng, e=e, j=j, bank=bank):
                        ins = None
                        for r in range(4):
                            for cc in range(8):
                                u = 128 * e + 8 * j + cc
                                beta, c = (u % 128) // 16, u % 16
                                v = u
                                ins = eng.matmul(
                                    ps_big[32 * r:32 * r + 16,
                                           512 * bank + 64 * cc:512 * bank + 64 * cc + 64],
                                    wT[32 * r:32 * r + 16,
                                       512 * beta + 32 * c:512 * beta + 32 * c + 16],
                                    vs3[32 * r:32 * r + 16, :, v],
                                    start=True, stop=True, tile_position=(32 * r, 32 * r),
                                    skip_group_check=True)
                        return ins

                    avt = pl.pe(fav, waits=W(tr[beta_of_j], *ldv,
                                             av_bank_free.get(bank), t_nrm))
                    sti = j % 2
                    cpt = pl.dve(lambda eng, bank=bank, sti=sti: eng.tensor_copy(
                        avs[sti][:], ps_big[:, 512 * bank:512 * (bank + 1)]),
                        waits=W(avt, avs_free.get(sti)))
                    av_bank_free[bank] = cpt
                    u0 = 128 * e + 8 * j
                    ad4 = ad.rearrange("(r u) (h dp) -> r u h dp", r=4, h=16)
                    wr = []
                    for r in range(4):
                        out_ap = ad4[r, u0:u0 + 8].transpose([1, 0, 2])
                        in_ap = avs[sti][32 * r:32 * r + 16].rearrange(
                            "p (cc dp) -> p cc dp", dp=64)
                        wr.append(pl.dma("sync", out_ap, in_ap, waits=W(cpt,)))
                    avs_free[sti] = W(*wr)
                    av_out_toks.extend(wr)
                for beta in range(8):
                    if av_bank_free.get(beta) is not None:
                        qk_bank_free[beta] = av_bank_free[beta]

            attn_done = W(*av_out_toks)
            pl.markers["p2"] = (len(pl.steps), attn_done)

        with contextlib.ExitStack() as st3:
            at_tm = st3.enter_context(nc.sbuf_tensor("at_tm", [128, 16 * 1024], BF16))
            afm = st3.enter_context(nc.sbuf_tensor("afm", [128, 8 * TC], BF16))
            xs_c = st3.enter_context(nc.sbuf_tensor("xs_c", [128, 8 * 512], F32))
            tmp = st3.enter_context(nc.sbuf_tensor("tmp", [128, 512], F32))
            ps_t = st3.enter_context(nc.psum_tensor("ps_t", [128, 2048], BF16))
            ps_mm2 = st3.enter_context(nc.psum_tensor("ps_mm2", [128, 2048], F32))

            ldat = pl.dma("sync", at_tm[:].rearrange("p (tt f) -> p tt f", tt=16),
                          ad.rearrange("(tt p) f -> p tt f", p=128), waits=W(attn_done))
            pl.markers["p3ld"] = (len(pl.steps), ldat)
            at3 = at_tm[:].rearrange("p (tt f) -> p tt f", tt=16)
            afm3 = afm[:].rearrange("p (k t) -> p k t", k=8)
            slot_free = {}
            tr_toks = []
            for tt in range(16):
                for fc in range(8):
                    idx = tt * 8 + fc
                    slot = idx % 16
                    col = (slot % 2) * 1024 + (slot // 2) * 128
                    pslot = ps_t[:, col:col + 128]
                    throttle = tr_toks[idx - 2] if idx >= 2 else None
                    pt = pl.pe(lambda eng, tt=tt, fc=fc, pslot=pslot: eng.transpose(
                        pslot, at3[:, tt, 128 * fc:128 * (fc + 1)], eye_sb[:]),
                        waits=W(ldat, ldeye, slot_free.get(slot), throttle))
                    ct = pl.dve(lambda eng, tt=tt, fc=fc, pslot=pslot: eng.tensor_copy(
                        afm3[:, fc, 128 * tt:128 * (tt + 1)], pslot), waits=W(pt,))
                    slot_free[slot] = ct
                    tr_toks.append(ct)
                    if len(tr_toks) == 8:
                        pl.markers["p3t8"] = (len(pl.steps), W(*tr_toks))
            afm_done = W(*tr_toks)
            pl.markers["p3a"] = (len(pl.steps), afm_done)

            wo3 = wo_sb[:].rearrange("p (k m c) -> p k m c", k=8, m=8)
            out_toks = []
            xs3 = xs_c[:].rearrange("p (k t) -> p k t", k=8)
            bank_free2 = {}
            tmp_free = None
            for ch in range(4):
                ldx = pl.dma("sync", xs3[:, :, :], x3[:, :, ch * 512:(ch + 1) * 512],
                             waits=W(*out_toks[-8:]) if out_toks else ())
                adds = []
                for m in range(8):
                    bank = m % 4
                    pslice = ps_mm2[:, 512 * bank:512 * (bank + 1)]

                    def fgrp(eng, m=m, ch=ch, pslice=pslice):
                        ins = None
                        for k in range(8):
                            ins = eng.matmul(pslice, wo3[:, k, m, :],
                                             afm3[:, k, ch * 512:(ch + 1) * 512],
                                             start=(k == 0), stop=(k == 7))
                        return ins

                    mm = pl.pe(fgrp, waits=W(afm_done, ldwo, bank_free2.get(bank)))
                    at_ = pl.act(lambda eng, m=m, pslice=pslice: eng.activation(
                        tmp[:], pslice, AF.Identity, bias=bo_sb[:, m:m + 1]),
                        waits=W(mm, ldbo, tmp_free))
                    ad_ = pl.dve(lambda eng, m=m: eng.tensor_add(
                        xs3[:, m, :], xs3[:, m, :], tmp[:]), waits=W(at_, ldx))
                    tmp_free = ad_
                    bank_free2[bank] = at_
                    adds.append(ad_)
                wr = pl.dma("sync", xo3[:, :, ch * 512:(ch + 1) * 512], xs3[:, :, :],
                            waits=W(*adds))
                out_toks.append(wr)
                pl.markers[f"p3ch{ch}"] = (len(pl.steps), wr)
            final = W(*out_toks)

        fin = st.enter_context(nc.sbuf_tensor("fin", [1, 1], F32))
        if upto is not None:
            nsteps, tok = pl.markers[upto]
            pl.steps = pl.steps[:nsteps]
            pl.dve(lambda eng: eng.memset(fin[:], 0.0), waits=W(tok))
        else:
            pl.dve(lambda eng: eng.memset(fin[:], 0.0), waits=W(final))
        run_plan(nc, pl)
    return nc


def build_ffn0():
    nc = bass.Bass("TRN2", target_bir_lowering=False, debug=False)
    x = nc.dram_tensor("x", [D, TC], F32, kind="ExternalInput").ap()
    w1 = nc.dram_tensor("w1", [D, 4096], F32, kind="ExternalInput").ap()
    s1 = nc.dram_tensor("s1", [128, 32], F32, kind="ExternalInput").ap()
    w2 = nc.dram_tensor("w2", [D, 4096], F32, kind="ExternalInput").ap()
    s2 = nc.dram_tensor("s2", [128, 32], F32, kind="ExternalInput").ap()
    h1 = nc.dram_tensor("h1", [4096, 4096], F32, kind="ExternalInput").ap()
    sh1 = nc.dram_tensor("sh1", [128, 32], F32, kind="ExternalInput").ap()
    h2 = nc.dram_tensor("h2", [4096, 4096], F32, kind="ExternalInput").ap()
    sh2 = nc.dram_tensor("sh2", [128, 32], F32, kind="ExternalInput").ap()
    w3 = nc.dram_tensor("w3", [4096, D], F32, kind="ExternalInput").ap()
    s3 = nc.dram_tensor("s3", [128, 8], F32, kind="ExternalInput").ap()
    xo = nc.dram_tensor("xo", [D, TC], F32, kind="ExternalOutput").ap()

    pl = Plan()
    x3 = x.rearrange("(k p) t -> p k t", p=128)
    xo3 = xo.rearrange("(k p) t -> p k t", p=128)
    w1v = w1.rearrange("(k p) (m c) -> m p k c", p=128, m=32)
    w2v = w2.rearrange("(k p) (m c) -> m p k c", p=128, m=32)
    h1v = h1.rearrange("(k p) (m c) -> m p k c", p=128, m=32)
    h2v = h2.rearrange("(k p) (m c) -> m p k c", p=128, m=32)
    w3v = w3.rearrange("(k p) (m c) -> m p k c", p=128, m=8)

    with contextlib.ExitStack() as st:
        xh = st.enter_context(nc.sbuf_tensor("xh", [128, 8 * TC], BF16))
        hq = st.enter_context(nc.sbuf_tensor("hq", [128, 32 * 512], BF16))
        gq = st.enter_context(nc.sbuf_tensor("gq", [128, 32 * 512], BF16))
        sA = st.enter_context(nc.sbuf_tensor("sA", [128, 512], F32))
        sB = st.enter_context(nc.sbuf_tensor("sB", [128, 512], F32))
        xsc = st.enter_context(nc.sbuf_tensor("xsc", [128, 8 * 512], F32))
        scl = st.enter_context(nc.sbuf_tensor("scl", [128, 32 * 5], F32))
        wsm = [st.enter_context(nc.sbuf_tensor(f"wsm{i}", [128, 8 * 128], BF16))
               for i in range(4)]
        wbg = [st.enter_context(nc.sbuf_tensor(f"wbg{i}", [128, 32 * 128], BF16))
               for i in range(3)]
        sc = alloc_ln_scratch(st, nc)
        ps_ln = st.enter_context(nc.psum_tensor("ps_ln", [128, 2048], F32))
        ps_mm = st.enter_context(nc.psum_tensor("ps_mm", [128, 2048], F32))
        fin = st.enter_context(nc.sbuf_tensor("fin", [1, 1], F32))

        scl3 = scl[:].rearrange("p (n m) -> p n m", n=5)
        lds = [pl.dma("sync", scl3[:, 0, :], s1), pl.dma("sync", scl3[:, 1, :], s2),
               pl.dma("sync", scl3[:, 2, :], sh1), pl.dma("sync", scl3[:, 3, :], sh2),
               pl.dma("sync", scl3[:, 4, 0:8], s3)]
        ldsw = W(*lds)
        cdep = emit_ln_setup(pl, sc)
        ln_toks, ln_all = emit_ln(pl, x3, xh, sc, ps_ln, cdep)
        xh3 = xh[:].rearrange("p (k t) -> p k t", k=8)
        hq3 = hq[:].rearrange("p (m t) -> p m t", m=32)
        gq3 = gq[:].rearrange("p (m t) -> p m t", m=32)
        xs3 = xsc[:].rearrange("p (k t) -> p k t", k=8)

        wsm_free = {}
        wbg_free = {}
        bank_free = {}
        sA_free = None
        sB_free = None
        out_wr = []
        for qtr in range(4):
            t0 = qtr * 512
            h_toks = []
            for m in range(32):
                bw1 = wsm[(2 * m) % 4]
                bw2 = wsm[(2 * m + 1) % 4]
                lw1 = pl.dma("gpsimd", bw1[:].rearrange("p (k c) -> p k c", k=8),
                             w1v[m], waits=W(wsm_free.get((2 * m) % 4)))
                lw2 = pl.dma("gpsimd", bw2[:].rearrange("p (k c) -> p k c", k=8),
                             w2v[m], waits=W(wsm_free.get((2 * m + 1) % 4)))
                pA = ps_mm[:, (2 * m % 4) * 512:(2 * m % 4) * 512 + 512]
                pB = ps_mm[:, ((2 * m + 1) % 4) * 512:((2 * m + 1) % 4) * 512 + 512]

                def f1(eng, m=m, t0=t0, bw1=bw1, pA=pA):
                    ins = None
                    b3 = bw1[:].rearrange("p (k c) -> p k c", k=8)
                    for k in range(8):
                        ins = eng.matmul(pA, b3[:, k, :], xh3[:, k, t0:t0 + 512],
                                         start=(k == 0), stop=(k == 7))
                    return ins

                def f2(eng, m=m, t0=t0, bw2=bw2, pB=pB):
                    ins = None
                    b3 = bw2[:].rearrange("p (k c) -> p k c", k=8)
                    for k in range(8):
                        ins = eng.matmul(pB, b3[:, k, :], xh3[:, k, t0:t0 + 512],
                                         start=(k == 0), stop=(k == 7))
                    return ins

                mmA = pl.pe(f1, waits=W(lw1, ln_all, bank_free.get((2 * m) % 4)))
                mmB = pl.pe(f2, waits=W(lw2, bank_free.get((2 * m + 1) % 4)))
                aS = pl.act(lambda eng, m=m, pA=pA: eng.activation(
                    sA[:], pA, AF.Silu, scale=scl3[:, 0, m:m + 1]),
                    waits=W(mmA, ldsw, sA_free))
                dS = pl.dve(lambda eng, m=m, pB=pB: eng.tensor_scalar_mul(
                    sB[:], pB, scl3[:, 1, m:m + 1]), waits=W(mmB, ldsw, sB_free))
                hT = pl.dve(lambda eng, m=m: eng.tensor_mul(
                    hq3[:, m, :], sA[:], sB[:]), waits=W(aS, dS))
                sA_free = hT
                sB_free = hT
                bank_free[(2 * m) % 4] = aS
                bank_free[(2 * m + 1) % 4] = dS
                wsm_free[(2 * m) % 4] = mmA
                wsm_free[(2 * m + 1) % 4] = mmB
                h_toks.append(hT)
            h_all = W(*h_toks)
            g_toks = []
            for m in range(32):
                wb = wbg[m % 3]
                lw = pl.dma("gpsimd", wb[:].rearrange("p (k c) -> p k c", k=32),
                            h1v[m], waits=W(wbg_free.get(m % 3)))
                bank = m % 4
                pA = ps_mm[:, bank * 512:bank * 512 + 512]

                def fh(eng, wb=wb, pA=pA):
                    ins = None
                    b3 = wb[:].rearrange("p (k c) -> p k c", k=32)
                    for k in range(32):
                        ins = eng.matmul(pA, b3[:, k, :], hq3[:, k, :],
                                         start=(k == 0), stop=(k == 31))
                    return ins

                mm_ = pl.pe(fh, waits=W(lw, h_all, bank_free.get(bank)))
                a_ = pl.act(lambda eng, m=m, pA=pA: eng.activation(
                    gq3[:, m, :], pA, AF.Gelu, scale=scl3[:, 2, m:m + 1]),
                    waits=W(mm_, ldsw))
                bank_free[bank] = a_
                wbg_free[m % 3] = mm_
                g_toks.append(a_)
            g_all = W(*g_toks)
            h2_toks = []
            for m in range(32):
                wb = wbg[m % 3]
                lw = pl.dma("gpsimd", wb[:].rearrange("p (k c) -> p k c", k=32),
                            h2v[m], waits=W(wbg_free.get(m % 3)))
                bank = m % 4
                pA = ps_mm[:, bank * 512:bank * 512 + 512]

                def fh2(eng, wb=wb, pA=pA):
                    ins = None
                    b3 = wb[:].rearrange("p (k c) -> p k c", k=32)
                    for k in range(32):
                        ins = eng.matmul(pA, b3[:, k, :], gq3[:, k, :],
                                         start=(k == 0), stop=(k == 31))
                    return ins

                mm_ = pl.pe(fh2, waits=W(lw, g_all, bank_free.get(bank), h_all))
                a_ = pl.act(lambda eng, m=m, pA=pA: eng.activation(
                    hq3[:, m, :], pA, AF.Gelu, scale=scl3[:, 3, m:m + 1]),
                    waits=W(mm_, ldsw))
                bank_free[bank] = a_
                wbg_free[m % 3] = mm_
                h2_toks.append(a_)
            h2_all = W(*h2_toks)
            ldx = pl.dma("sync", xs3[:, :, :], x3[:, :, t0:t0 + 512],
                         waits=W(*out_wr))
            adds = []
            tmp_free = sA_free
            for m in range(8):
                wb = wbg[m % 3]
                lw = pl.dma("gpsimd", wb[:].rearrange("p (k c) -> p k c", k=32),
                            w3v[m], waits=W(wbg_free.get(m % 3)))
                bank = m % 4
                pA = ps_mm[:, bank * 512:bank * 512 + 512]

                def fw3(eng, wb=wb, pA=pA):
                    ins = None
                    b3 = wb[:].rearrange("p (k c) -> p k c", k=32)
                    for k in range(32):
                        ins = eng.matmul(pA, b3[:, k, :], hq3[:, k, :],
                                         start=(k == 0), stop=(k == 31))
                    return ins

                mm_ = pl.pe(fw3, waits=W(lw, h2_all, bank_free.get(bank)))
                a_ = pl.act(lambda eng, m=m, pA=pA: eng.activation(
                    sA[:], pA, AF.Identity, scale=scl3[:, 4, m:m + 1]),
                    waits=W(mm_, ldsw, tmp_free))
                d_ = pl.dve(lambda eng, m=m: eng.tensor_add(
                    xs3[:, m, :], xs3[:, m, :], sA[:]), waits=W(a_, ldx))
                tmp_free = d_
                bank_free[bank] = a_
                wbg_free[m % 3] = mm_
                adds.append(d_)
            wr = pl.dma("sync", xo3[:, :, t0:t0 + 512], xs3[:, :, :], waits=W(*adds))
            out_wr = [wr]
            sA_free = tmp_free
        final = W(*out_wr)
        pl.dve(lambda eng: eng.memset(fin[:], 0.0), waits=W(final))
        run_plan(nc, pl)
    return nc


def build_ablocal():
    nc = bass.Bass("TRN2", target_bir_lowering=False, debug=False)
    x = nc.dram_tensor("x", [D, TC], F32, kind="ExternalInput").ap()
    wqb = nc.dram_tensor("wqb", [D, 64], F32, kind="ExternalInput").ap()
    wkb = nc.dram_tensor("wkb", [D, 64], F32, kind="ExternalInput").ap()
    wvb = nc.dram_tensor("wvb", [D, 64], F32, kind="ExternalInput").ap()
    mask = nc.dram_tensor("mask", [128, 128], F32, kind="ExternalInput").ap()
    ou = nc.dram_tensor("ou", [64, TC], F32, kind="ExternalOutput").ap()
    deno = nc.dram_tensor("deno", [1, TC], F32, kind="ExternalOutput").ap()
    So = nc.dram_tensor("So", [64, 64], F32, kind="ExternalOutput").ap()
    zo = nc.dram_tensor("zo", [64, 1], F32, kind="ExternalOutput").ap()
    pqo = nc.dram_tensor("pqo", [64, TC], F32, kind="ExternalOutput").ap()

    pl = Plan()
    x3 = x.rearrange("(k p) t -> p k t", p=128)

    with contextlib.ExitStack() as st:
        xh = st.enter_context(nc.sbuf_tensor("xh", [128, 8 * TC], BF16))
        wq_sb = st.enter_context(nc.sbuf_tensor("wq_sb", [128, 8 * 64], BF16))
        wk_sb = st.enter_context(nc.sbuf_tensor("wk_sb", [128, 8 * 64], BF16))
        wv_sb = st.enter_context(nc.sbuf_tensor("wv_sb", [128, 8 * 64], BF16))
        msk = st.enter_context(nc.sbuf_tensor("msk", [128, 128], F32))
        pq = st.enter_context(nc.sbuf_tensor("pq", [64, TC], BF16))
        pqf = st.enter_context(nc.sbuf_tensor("pqf", [64, TC], F32))
        pk = st.enter_context(nc.sbuf_tensor("pk", [64, TC], BF16))
        vf = st.enter_context(nc.sbuf_tensor("vf", [64, TC], BF16))
        pkT = st.enter_context(nc.sbuf_tensor("pkT", [128, 16 * 64], BF16))
        vT = st.enter_context(nc.sbuf_tensor("vT", [128, 16 * 64], BF16))
        t1 = st.enter_context(nc.sbuf_tensor("t1", [128, 512], F32))
        t2 = st.enter_context(nc.sbuf_tensor("t2", [128, 512], F32))
        Avm = st.enter_context(nc.sbuf_tensor("Avm", [128, 128], BF16))
        Akm = st.enter_context(nc.sbuf_tensor("Akm", [128, 128], BF16))
        S_sb = st.enter_context(nc.sbuf_tensor("S_sb", [64, 64], BF16))
        z_sb = st.enter_context(nc.sbuf_tensor("z_sb", [64, 1], BF16))
        ou_sb = st.enter_context(nc.sbuf_tensor("ou_sb", [64, TC], F32))
        den_sb = st.enter_context(nc.sbuf_tensor("den_sb", [1, TC], F32))
        Sf = st.enter_context(nc.sbuf_tensor("Sf", [64, 64], F32))
        zf = st.enter_context(nc.sbuf_tensor("zf", [64, 1], F32))
        sc = alloc_ln_scratch(st, nc)
        ps = st.enter_context(nc.psum_tensor("ps", [128, 2048], F32))
        fin = st.enter_context(nc.sbuf_tensor("fin", [1, 1], F32))

        ldw = [pl.dma("gpsimd", wq_sb[:].rearrange("p (k c) -> p k c", k=8),
                      wqb.rearrange("(k p) c -> p k c", p=128)),
               pl.dma("gpsimd", wk_sb[:].rearrange("p (k c) -> p k c", k=8),
                      wkb.rearrange("(k p) c -> p k c", p=128)),
               pl.dma("gpsimd", wv_sb[:].rearrange("p (k c) -> p k c", k=8),
                      wvb.rearrange("(k p) c -> p k c", p=128)),
               pl.dma("sync", msk[:], mask)]
        cdep = emit_ln_setup(pl, sc)
        with nc.psum_tensor("ps_ln", [128, 2048], F32) as ps_ln:
            ln_toks, ln_all = emit_ln(pl, x3, xh, sc, ps_ln, cdep)
        ps2 = st.enter_context(nc.psum_tensor("ps2", [128, 1024], F32))
        xh3 = xh[:].rearrange("p (k t) -> p k t", k=8)
        wq3 = wq_sb[:].rearrange("p (k c) -> p k c", k=8)
        wk3 = wk_sb[:].rearrange("p (k c) -> p k c", k=8)
        wv3 = wv_sb[:].rearrange("p (k c) -> p k c", k=8)
        pkT3 = pkT[:].rearrange("p (tt c) -> p tt c", tt=16)
        vT3 = vT[:].rearrange("p (tt c) -> p tt c", tt=16)

        def phi_emit(dst_bf, dst_f32, ps_ap, ch, dep):
            a = pl.dve(lambda eng: eng.tensor_scalar_min(t1[:64, :], ps_ap, 0.0),
                       waits=W(dep, pl._t1f))
            b = pl.act(lambda eng: eng.activation(t1[:64, :], t1[:64, :], AF.Exp),
                       waits=W(a,))
            c = pl.dve(lambda eng: eng.tensor_scalar_max(t2[:64, :], ps_ap, 0.0),
                       waits=W(dep, pl._t2f))
            d = pl.dve(lambda eng, ch=ch: eng.tensor_add(
                dst_f32[:, ch * 512:(ch + 1) * 512] if dst_f32 is not None else t2[:64, :],
                t1[:64, :], t2[:64, :]), waits=W(b, c))
            src = (dst_f32[:, ch * 512:(ch + 1) * 512] if dst_f32 is not None else t2[:64, :])
            e = pl.dve(lambda eng, ch=ch, src=src: eng.tensor_copy(
                dst_bf[:, ch * 512:(ch + 1) * 512], src), waits=W(d,))
            pl._t1f = e
            pl._t2f = e
            return e

        pl._t1f = None
        pl._t2f = None
        pq_toks, pk_toks, vf_toks = [], [], []
        bank_free = {}
        for ch in range(4):
            for (name, w3_, dbf, df32, toks) in (
                    ("q", wq3, pq, pqf, pq_toks), ("k", wk3, pk, None, pk_toks)):
                bank = len(toks) % 2
                pA = ps[0:64, bank * 512:bank * 512 + 512]

                def fp(eng, w3_=w3_, ch=ch, pA=pA):
                    ins = None
                    for k in range(8):
                        ins = eng.matmul(pA, w3_[:, k, :], xh3[:, k, ch * 512:(ch + 1) * 512],
                                         start=(k == 0), stop=(k == 7))
                    return ins

                mm_ = pl.pe(fp, waits=W(*ldw, ln_all, bank_free.get(("p", bank))))
                e = phi_emit(dbf, df32, pA, ch, mm_)
                bank_free[("p", bank)] = e
                toks.append(e)
            bank = 2
            pA = ps[0:64, bank * 512:bank * 512 + 512]

            def fv(eng, ch=ch, pA=pA):
                ins = None
                for k in range(8):
                    ins = eng.matmul(pA, wv3[:, k, :], xh3[:, k, ch * 512:(ch + 1) * 512],
                                     start=(k == 0), stop=(k == 7))
                return ins

            mm_ = pl.pe(fv, waits=W(*ldw, ln_all, bank_free.get(("p", 2))))
            e = pl.dve(lambda eng, ch=ch, pA=pA: eng.tensor_copy(
                vf[:, ch * 512:(ch + 1) * 512], pA), waits=W(mm_, bank_free.get(("p2", 2))))
            bank_free[("p", 2)] = e
            vf_toks.append(e)
        bank_free[("t", 0)] = bank_free.get(("p", 0))
        bank_free[("t", 2)] = bank_free.get(("p", 2))
        pkT_toks, vT_toks = [], []
        for tt in range(16):
            for (dst3, wmat, toks, dophi) in ((pkT3, wk3, pkT_toks, True),
                                              (vT3, wv3, vT_toks, False)):
                bank = 0 if dophi else 2

                pA = ps[:, bank * 512:bank * 512 + 64]

                def ft(eng, tt=tt, wmat=wmat, pA=pA):
                    ins = None
                    for k in range(8):
                        ins = eng.matmul(pA, xh3[:, k, tt * 128:(tt + 1) * 128],
                                         wmat[:, k, :], start=(k == 0), stop=(k == 7))
                    return ins

                mm_ = pl.pe(ft, waits=W(*ldw, ln_all, bank_free.get(("t", bank))))
                if dophi:
                    a = pl.dve(lambda eng, pA=pA: eng.tensor_scalar_min(t1[:, 0:64], pA, 0.0),
                               waits=W(mm_, pl._t1f))
                    b = pl.act(lambda eng: eng.activation(t1[:, 0:64], t1[:, 0:64], AF.Exp),
                               waits=W(a,))
                    c = pl.dve(lambda eng, pA=pA: eng.tensor_scalar_max(t2[:, 0:64], pA, 0.0),
                               waits=W(mm_, pl._t2f))
                    d = pl.dve(lambda eng, tt=tt: eng.tensor_add(t2[:, 0:64], t1[:, 0:64], t2[:, 0:64]),
                               waits=W(b, c))
                    e = pl.dve(lambda eng, tt=tt, dst3=dst3: eng.tensor_copy(
                        dst3[:, tt, :], t2[:, 0:64]), waits=W(d,))
                    pl._t1f = e
                    pl._t2f = e
                else:
                    e = pl.dve(lambda eng, tt=tt, dst3=dst3, pA=pA: eng.tensor_copy(
                        dst3[:, tt, :], pA), waits=W(mm_,))
                bank_free[("t", bank)] = e
                toks.append(e)
        pq_all = W(*pq_toks)
        pk_all = W(*pk_toks)
        vf_all = W(*vf_toks)
        pkT_all = W(*pkT_toks)
        vT_all = W(*vT_toks)
        pre_all = W(pq_all, pk_all, vf_all, pkT_all, vT_all)

        z0 = pl.dve(lambda eng: eng.memset(S_sb[:], 0.0))
        z1 = pl.dve(lambda eng: eng.memset(z_sb[:], 0.0))

        psAv = ps[:, 0:128]
        psAk = ps[:, 512:640]
        psO = ps[0:64, 1024:1152]
        psDen = ps[0:1, 1536:1664]
        psS = ps2[0:64, 0:64]
        psZ = ps2[0:64, 512:513]

        S_upd = None
        z_upd = None
        o_toks = []
        den_toks = []
        copyS = copyZ = None
        psAv_free = psAk_free = None
        avmbuf_free = akmbuf_free = None
        for cchunk in range(16):
            tt = cchunk
            if cchunk > 0:
                copyS = pl.dve(lambda eng: eng.tensor_copy(S_sb[:], psS), waits=W(S_upd,))
                copyZ = pl.dve(lambda eng: eng.tensor_copy(z_sb[:], psZ), waits=W(z_upd,))
            mmAv = pl.pe(lambda eng, tt=tt: eng.matmul(
                psAv, vf[:, tt * 128:(tt + 1) * 128], pq[:, tt * 128:(tt + 1) * 128],
                start=True, stop=True), waits=W(pre_all, psAv_free))
            mmAk = pl.pe(lambda eng, tt=tt: eng.matmul(
                psAk, pk[:, tt * 128:(tt + 1) * 128], pq[:, tt * 128:(tt + 1) * 128],
                start=True, stop=True), waits=W(psAk_free,))
            eAv = pl.dve(lambda eng: eng.tensor_mul(Avm[:], psAv, msk[:]),
                         waits=W(mmAv, avmbuf_free))
            eAk = pl.dve(lambda eng: eng.tensor_mul(Akm[:], psAk, msk[:]),
                         waits=W(mmAk, akmbuf_free))
            psAv_free = eAv
            psAk_free = eAk
            def fo1(eng, tt=tt):
                return eng.matmul(psO, pkT3[:, tt, :], Avm[:], start=True, stop=False)
            def fo2(eng, tt=tt):
                return eng.matmul(psO, S_sb[:], pq[:, tt * 128:(tt + 1) * 128],
                                  start=False, stop=True)
            mo1 = pl.pe(fo1, waits=W(eAv, o_toks[-1] if o_toks else None))
            mo2 = pl.pe(fo2, waits=W(copyS, z0))
            avmbuf_free = mo1
            def fd1(eng, tt=tt):
                return eng.matmul(psDen, sc["ones_bf"][:], Akm[:], start=True, stop=False)
            def fd2(eng, tt=tt):
                return eng.matmul(psDen, z_sb[:], pq[:, tt * 128:(tt + 1) * 128],
                                  start=False, stop=True)
            md1 = pl.pe(fd1, waits=W(eAk, den_toks[-1] if den_toks else None))
            md2 = pl.pe(fd2, waits=W(copyZ, z1))
            akmbuf_free = md1
            def fS(eng, tt=tt):
                return eng.matmul(psS, vT3[:, tt, :], pkT3[:, tt, :],
                                  start=(tt == 0), stop=True, skip_group_check=True)
            def fZ(eng, tt=tt):
                return eng.matmul(psZ, pkT3[:, tt, :], sc["ones_bf"][:128, :],
                                  start=(tt == 0), stop=True, skip_group_check=True)
            S_upd = pl.pe(fS, waits=W(copyS, pkT_all, vT_all))
            z_upd = pl.pe(fZ, waits=W(copyZ, pkT_all))
            eo = pl.dve(lambda eng, tt=tt: eng.tensor_copy(
                ou_sb[:, tt * 128:(tt + 1) * 128], psO), waits=W(mo1, mo2))
            ed = pl.dve(lambda eng, tt=tt: eng.tensor_copy(
                den_sb[:, tt * 128:(tt + 1) * 128], psDen), waits=W(md1, md2))
            o_toks.append(eo)
            den_toks.append(ed)
        cS = pl.dve(lambda eng: eng.tensor_copy(Sf[:], psS), waits=W(S_upd,))
        cZ = pl.dve(lambda eng: eng.tensor_copy(zf[:], psZ), waits=W(z_upd,))
        wrs = [pl.dma("sync", ou, ou_sb[:], waits=W(*o_toks)),
               pl.dma("sync", deno, den_sb[:], waits=W(*den_toks)),
               pl.dma("sync", So, Sf[:], waits=W(cS,)),
               pl.dma("sync", zo, zf[:], waits=W(cZ,)),
               pl.dma("sync", pqo, pqf[:], waits=W(pq_all,))]
        final = W(*wrs)
        pl.dve(lambda eng: eng.memset(fin[:], 0.0), waits=W(final))
        run_plan(nc, pl)
    return nc


def build_abfin(upto=None):
    nc = bass.Bass("TRN2", target_bir_lowering=False, debug=False)
    x = nc.dram_tensor("x", [D, TC], F32, kind="ExternalInput").ap()
    ou = nc.dram_tensor("ou", [64, TC], F32, kind="ExternalInput").ap()
    denr = nc.dram_tensor("denr", [1, TC], F32, kind="ExternalInput").ap()
    pq = nc.dram_tensor("pq", [64, TC], F32, kind="ExternalInput").ap()
    Sstk = nc.dram_tensor("Sstk", [64, 192], F32, kind="ExternalInput").ap()
    zstk = nc.dram_tensor("zstk", [64, 3], F32, kind="ExternalInput").ap()
    wob = nc.dram_tensor("wob", [64, D], F32, kind="ExternalInput").ap()
    xo = nc.dram_tensor("xo", [D, TC], F32, kind="ExternalOutput").ap()

    pl = Plan()
    x3 = x.rearrange("(k p) t -> p k t", p=128)
    xo3 = xo.rearrange("(k p) t -> p k t", p=128)

    with contextlib.ExitStack() as st:
        pqb = st.enter_context(nc.sbuf_tensor("pqb", [64, TC], BF16))
        ouf = st.enter_context(nc.sbuf_tensor("ouf", [64, TC], F32))
        dfl = st.enter_context(nc.sbuf_tensor("dfl", [1, TC], F32))
        Ss = st.enter_context(nc.sbuf_tensor("Ss", [64, 192], F32))
        zs = st.enter_context(nc.sbuf_tensor("zs", [64, 3], F32))
        Spb = st.enter_context(nc.sbuf_tensor("Spb", [64, 64], BF16))
        zpb = st.enter_context(nc.sbuf_tensor("zpb", [64, 1], BF16))
        wo_sb = st.enter_context(nc.sbuf_tensor("wo_sb", [64, D], BF16))
        onb = st.enter_context(nc.sbuf_tensor("onb", [64, TC], BF16))
        ones_r = st.enter_context(nc.sbuf_tensor("ones_r", [1, 64], F32))
        xsc = st.enter_context(nc.sbuf_tensor("xsc", [128, 8 * 512], F32))
        ps = st.enter_context(nc.psum_tensor("ps", [128, 2048], F32))
        fin = st.enter_context(nc.sbuf_tensor("fin", [1, 1], F32))

        ld = [pl.dma("gpsimd", pqb[:], pq),
              pl.dma("sync", ouf[:], ou),
              pl.dma("sync", dfl[:], denr),
              pl.dma("sync", Ss[:], Sstk),
              pl.dma("sync", zs[:], zstk),
              pl.dma("gpsimd", wo_sb[:], wob)]
        c0 = pl.dve(lambda eng: eng.memset(ones_r[:], 1.0))
        a1 = pl.dve(lambda eng: eng.tensor_add(Ss[:, 0:64], Ss[:, 0:64], Ss[:, 64:128]),
                    waits=W(ld[3],))
        a2 = pl.dve(lambda eng: eng.tensor_add(Ss[:, 0:64], Ss[:, 0:64], Ss[:, 128:192]),
                    waits=W(a1,))
        a3 = pl.dve(lambda eng: eng.tensor_copy(Spb[:], Ss[:, 0:64]), waits=W(a2,))
        b1 = pl.dve(lambda eng: eng.tensor_add(zs[:, 0:1], zs[:, 0:1], zs[:, 1:2]),
                    waits=W(ld[4],))
        b2 = pl.dve(lambda eng: eng.tensor_add(zs[:, 0:1], zs[:, 0:1], zs[:, 2:3]),
                    waits=W(b1,))
        b3 = pl.dve(lambda eng: eng.tensor_copy(zpb[:], zs[:, 0:1]), waits=W(b2,))

        ocorr_toks = []
        dtoks = []
        psO_free = psD_free = None
        for tt in range(16):
            psO = ps[0:64, (tt % 3) * 512:(tt % 3) * 512 + 128]
            psDn = ps[0:1, 1664 + (tt % 2) * 128: 1664 + (tt % 2) * 128 + 128]
            mo = pl.pe(lambda eng, tt=tt, psO=psO: eng.matmul(
                psO, Spb[:], pqb[:, tt * 128:(tt + 1) * 128], start=True, stop=True),
                waits=W(a3, ld[0], psO_free))
            md = pl.pe(lambda eng, tt=tt, psDn=psDn: eng.matmul(
                psDn, zpb[:], pqb[:, tt * 128:(tt + 1) * 128], start=True, stop=True),
                waits=W(b3, psD_free))
            eo = pl.dve(lambda eng, tt=tt, psO=psO: eng.tensor_add(
                ouf[:, tt * 128:(tt + 1) * 128], ouf[:, tt * 128:(tt + 1) * 128], psO),
                waits=W(mo, ld[1]))
            ed = pl.dve(lambda eng, tt=tt, psDn=psDn: eng.tensor_add(
                dfl[:, tt * 128:(tt + 1) * 128], dfl[:, tt * 128:(tt + 1) * 128], psDn),
                waits=W(md, ld[2]))
            psO_free = eo
            psD_free = ed
            ocorr_toks.append(eo)
            dtoks.append(ed)
        pl.markers = getattr(pl, "markers", {})
        pl.markers["f1"] = (len(pl.steps), W(*ocorr_toks, *dtoks))
        r1 = pl.dve(lambda eng: eng.tensor_scalar_max(dfl[:], dfl[:], 1e-6),
                    waits=W(*dtoks))
        r2 = pl.dve(lambda eng: eng.reciprocal(dfl[:], dfl[:]), waits=W(r1,))
        n_toks = []
        rep_free = None
        for ch in range(4):
            psR = ps[0:64, 1024 + (ch % 2) * 512:1024 + (ch % 2) * 512 + 512]
            mr = pl.pe(lambda eng, ch=ch, psR=psR: eng.matmul(
                psR, ones_r[:], dfl[:, ch * 512:(ch + 1) * 512], start=True, stop=True),
                waits=W(r2, c0, rep_free))
            nt = pl.dve(lambda eng, ch=ch, psR=psR: eng.tensor_mul(
                onb[:, ch * 512:(ch + 1) * 512], ouf[:, ch * 512:(ch + 1) * 512], psR),
                waits=W(mr, *ocorr_toks))
            rep_free = nt
            n_toks.append(nt)
        pl.markers["f2"] = (len(pl.steps), W(*n_toks))
        out_wr = []
        bank_free = {}
        for ch in range(4):
            ldx = pl.dma("sync", xsc[:].rearrange("p (k t) -> p k t", k=8),
                         x3[:, :, ch * 512:(ch + 1) * 512], waits=W(*out_wr))
            adds = []
            for m in range(8):
                bank = m % 4
                pA = ps[:, bank * 512:bank * 512 + 512]
                mm_ = pl.pe(lambda eng, m=m, ch=ch, pA=pA: eng.matmul(
                    pA, wo_sb[:, m * 128:(m + 1) * 128], onb[:, ch * 512:(ch + 1) * 512],
                    start=True, stop=True),
                    waits=W(*n_toks, ld[5], bank_free.get(bank)))
                d_ = pl.dve(lambda eng, m=m, pA=pA: eng.tensor_add(
                    xsc[:].rearrange("p (k t) -> p k t", k=8)[:, m, :],
                    xsc[:].rearrange("p (k t) -> p k t", k=8)[:, m, :], pA),
                    waits=W(mm_, ldx))
                bank_free[bank] = d_
                adds.append(d_)
            wr = pl.dma("sync", xo3[:, :, ch * 512:(ch + 1) * 512],
                        xsc[:].rearrange("p (k t) -> p k t", k=8), waits=W(*adds))
            out_wr = [wr]
        final = W(*out_wr)
        if upto is not None:
            nsteps, tok = pl.markers[upto]
            pl.steps = pl.steps[:nsteps]
            pl.dve(lambda eng: eng.memset(fin[:], 0.0), waits=W(tok))
        else:
            pl.dve(lambda eng: eng.memset(fin[:], 0.0), waits=W(final))
        run_plan(nc, pl)
    return nc


def build_fftlocal():
    nc = bass.Bass("TRN2", target_bir_lowering=False, debug=False)
    x = nc.dram_tensor("x", [D, TC], F32, kind="ExternalInput").ap()
    wq5 = nc.dram_tensor("wq5", [D, D], F32, kind="ExternalInput").ap()
    wk5 = nc.dram_tensor("wk5", [D, D], F32, kind="ExternalInput").ap()
    wv5 = nc.dram_tensor("wv5", [D, D], F32, kind="ExternalInput").ap()
    fwd = nc.dram_tensor("fwd", [D, 2 * FP], F32, kind="ExternalInput").ap()
    inv = nc.dram_tensor("inv", [2 * FP, D], F32, kind="ExternalInput").ap()
    xo = nc.dram_tensor("xo", [D, TC], F32, kind="ExternalOutput").ap()
    cfq = nc.dram_tensor("cfq", [2 * FP, TC], F32, kind="ExternalOutput").ap()
    tot = nc.dram_tensor("tot", [128, 10], F32, kind="ExternalOutput").ap()
    qdr = nc.dram_tensor("qdr", [D, TC], BF16, kind="Internal").ap()
    kdr = nc.dram_tensor("kdr", [D, TC], BF16, kind="Internal").ap()
    vdr = nc.dram_tensor("vdr", [D, TC], BF16, kind="Internal").ap()

    pl = Plan()
    x3 = x.rearrange("(k p) t -> p k t", p=128)
    xo3 = xo.rearrange("(k p) t -> p k t", p=128)
    cfq3 = cfq.rearrange("(j p) t -> p j t", p=128)
    fwdv = fwd.rearrange("(k p) (j c) -> j p k c", p=128, j=10)
    invv = inv.rearrange("(k p) (m c) -> m p k c", p=128, m=8)

    with contextlib.ExitStack() as st:
        ps = st.enter_context(nc.psum_tensor("ps", [128, 2048], F32))
        fin_t = st.enter_context(nc.sbuf_tensor("fin", [1, 1], F32))
        totsb = st.enter_context(nc.sbuf_tensor("totsb", [128, 10], F32))
        t1 = st.enter_context(nc.sbuf_tensor("t1", [128, 512], F32))
        t2 = st.enter_context(nc.sbuf_tensor("t2", [128, 512], F32))
        t3 = st.enter_context(nc.sbuf_tensor("t3", [128, 512], F32))
        t4 = st.enter_context(nc.sbuf_tensor("t4", [128, 512], F32))
        wdft = [st.enter_context(nc.sbuf_tensor(f"wdft{i}", [128, 8 * 128], BF16))
                for i in range(2)]
        sc = alloc_ln_scratch(st, nc)

        with contextlib.ExitStack() as s1:
            xh = s1.enter_context(nc.sbuf_tensor("xh", [128, 8 * TC], BF16))
            osb = s1.enter_context(nc.sbuf_tensor("osb", [128, 8 * TC], BF16))
            wpr = [s1.enter_context(nc.sbuf_tensor(f"wpr{i}", [128, 8 * 128], BF16))
                   for i in range(3)]
            with nc.psum_tensor("ps_ln", [128, 2048], F32) as ps_ln:
                cdep = emit_ln_setup(pl, sc)
                ln_toks, ln_all = emit_ln(pl, x3, xh, sc, ps_ln, cdep)
            xh3 = xh[:].rearrange("p (k t) -> p k t", k=8)
            osb3 = osb[:].rearrange("p (k t) -> p k t", k=8)
            spills = []
            bank_free = {}
            wfree = {}
            for (wi, (wdram, ddram)) in enumerate(((wq5, qdr), (wk5, kdr), (wv5, vdr))):
                wv_ = wdram.rearrange("(k p) (m c) -> m p k c", p=128, m=8)
                toks = []
                for m in range(8):
                    wb = wpr[(wi * 8 + m) % 3]
                    lw = pl.dma("gpsimd", wb[:].rearrange("p (k c) -> p k c", k=8),
                                wv_[m], waits=W(wfree.get((wi * 8 + m) % 3)))
                    for ch in range(4):
                        bank = (m * 4 + ch) % 4
                        pA = ps[:, bank * 512:bank * 512 + 512]

                        def fp(eng, ch=ch, wb=wb, pA=pA):
                            ins = None
                            b3 = wb[:].rearrange("p (k c) -> p k c", k=8)
                            for k in range(8):
                                ins = eng.matmul(pA, b3[:, k, :],
                                                 xh3[:, k, ch * 512:(ch + 1) * 512],
                                                 start=(k == 0), stop=(k == 7))
                            return ins

                        mm_ = pl.pe(fp, waits=W(lw, ln_all, bank_free.get(bank)))
                        ev = pl.dve(lambda eng, m=m, ch=ch, pA=pA: eng.tensor_copy(
                            osb3[:, m, ch * 512:(ch + 1) * 512], pA),
                            waits=W(mm_, spills[-1] if toks == [] and ch == 0 and m == 0 and spills else None))
                        bank_free[bank] = ev
                        wfree[(wi * 8 + m) % 3] = mm_
                        toks.append(ev)
                sp = pl.dma("sync", ddram.rearrange("(k p) t -> p k t", p=128), osb3,
                            waits=W(*toks))
                spills.append(sp)
            spill_all = W(*spills)

        prpi = st.enter_context(nc.sbuf_tensor("prpi", [128, 10 * TC], F32))
        prpi3 = prpi[:].rearrange("p (j t) -> p j t", j=10)
        dft_w_free = {}
        dft_bank_free = {}
        with contextlib.ExitStack() as s2:
            ksb = s2.enter_context(nc.sbuf_tensor("ksb", [128, 8 * TC], BF16))
            vsb = s2.enter_context(nc.sbuf_tensor("vsb", [128, 8 * TC], BF16))
            ksb3 = ksb[:].rearrange("p (k t) -> p k t", k=8)
            vsb3 = vsb[:].rearrange("p (k t) -> p k t", k=8)
            ldk = pl.dma("sync", ksb3, kdr.rearrange("(k p) t -> p k t", p=128),
                         waits=W(spill_all))
            ldv = pl.dma("sync", vsb3, vdr.rearrange("(k p) t -> p k t", p=128),
                         waits=W(spill_all))
            prod_toks = []
            scr_free = None
            for j in range(5):
                for ch in range(4):
                    pairs = []
                    for (srcv, src_dep, b0) in ((vsb3, ldv, 0), (ksb3, ldk, 2)):
                        for half in range(2):
                            tile = j + 5 * half
                            bank = b0 + half
                            slot = bank % 2
                            wb = wdft[slot]
                            lw = pl.dma("gpsimd", wb[:].rearrange("p (k c) -> p k c", k=8),
                                        fwdv[tile], waits=W(dft_w_free.get(slot)))
                            pA = ps[:, bank * 512:bank * 512 + 512]

                            def fd(eng, wb=wb, ch=ch, pA=pA, srcv=srcv):
                                ins = None
                                b3 = wb[:].rearrange("p (k c) -> p k c", k=8)
                                for k in range(8):
                                    ins = eng.matmul(pA, b3[:, k, :],
                                                     srcv[:, k, ch * 512:(ch + 1) * 512],
                                                     start=(k == 0), stop=(k == 7))
                                return ins

                            mm_ = pl.pe(fd, waits=W(lw, src_dep, dft_bank_free.get(bank)))
                            dft_w_free[slot] = mm_
                            pairs.append((mm_, pA))
                    (mvr, pvr), (mvi, pvi), (mkr, pkr), (mki, pki) = pairs
                    n1 = pl.act(lambda eng, pkr=pkr: eng.activation(t1[:], pkr, AF.Square),
                                waits=W(mkr, scr_free))
                    n2 = pl.act(lambda eng, pki=pki: eng.activation(t2[:], pki, AF.Square),
                                waits=W(mki,))
                    n3 = pl.dve(lambda eng: eng.tensor_add(t1[:], t1[:], t2[:]), waits=W(n1, n2))
                    n4 = pl.act(lambda eng: eng.activation(t1[:], t1[:], AF.Sqrt), waits=W(n3,))
                    n5 = pl.dve(lambda eng: eng.tensor_scalar_add(t1[:], t1[:], 1e-8), waits=W(n4,))
                    n6 = pl.dve(lambda eng: eng.reciprocal(t1[:], t1[:]), waits=W(n5,))
                    n7 = pl.dve(lambda eng, pkr=pkr: eng.tensor_mul(t3[:], pkr, t1[:]), waits=W(n6,))
                    n8 = pl.dve(lambda eng, pki=pki: eng.tensor_mul(t4[:], pki, t1[:]), waits=W(n7,))
                    dft_bank_free[2] = n7
                    dft_bank_free[3] = n8
                    u1 = pl.dve(lambda eng, pvr=pvr: eng.tensor_mul(t1[:], t3[:], pvr),
                                waits=W(n7, n8, mvr))
                    u2 = pl.dve(lambda eng, pvi=pvi: eng.tensor_mul(t2[:], t4[:], pvi),
                                waits=W(n8, mvi))
                    u3 = pl.dve(lambda eng, j=j, ch=ch: eng.tensor_sub(
                        prpi3[:, j, ch * 512:(ch + 1) * 512], t1[:], t2[:]), waits=W(u1, u2))
                    u4 = pl.dve(lambda eng, pvi=pvi: eng.tensor_mul(t1[:], t3[:], pvi),
                                waits=W(u3, mvi))
                    u5 = pl.dve(lambda eng, pvr=pvr: eng.tensor_mul(t2[:], t4[:], pvr),
                                waits=W(u4, mvr))
                    u6 = pl.dve(lambda eng, j=j, ch=ch: eng.tensor_add(
                        prpi3[:, j + 5, ch * 512:(ch + 1) * 512], t1[:], t2[:]), waits=W(u4, u5))
                    dft_bank_free[0] = u5
                    dft_bank_free[1] = u4
                    scr_free = u6
                    prod_toks.append(u6)
            prod_all = W(*prod_toks)
        zero1 = pl.dve(lambda eng: eng.memset(t1[:, 0:1], 0.0), waits=W(scr_free,))
        cum_toks = []
        tot_toks = []
        for j in range(10):
            ctk = pl.dve(lambda eng, j=j: eng.tensor_tensor_scan(
                prpi3[:, j, :], prpi3[:, j, :],
                t1[:, 0:1].broadcast_to([128, TC]), 0.0, ALU.add, ALU.add),
                waits=W(prod_all, zero1))
            tt_ = pl.dve(lambda eng, j=j: eng.tensor_copy(
                totsb[:, j:j + 1], prpi3[:, j, TC - 1:TC]), waits=W(ctk,))
            cum_toks.append(ctk)
            tot_toks.append(tt_)
        cum_all = W(*cum_toks)
        wr_tot = pl.dma("sync", tot, totsb[:], waits=W(*tot_toks))

        qn = st.enter_context(nc.sbuf_tensor("qn", [128, 10 * TC], BF16))
        qn3 = qn[:].rearrange("p (j t) -> p j t", j=10)
        qn_toks = []
        exp_toks = []
        with contextlib.ExitStack() as s3:
            qsb = s3.enter_context(nc.sbuf_tensor("qsb2", [128, 8 * TC], BF16))
            qsb3 = qsb[:].rearrange("p (k t) -> p k t", k=8)
            ldq = pl.dma("sync", qsb3, qdr.rearrange("(k p) t -> p k t", p=128),
                         waits=W(spill_all))
            scr_free2 = scr_free
            for j in range(5):
                for ch in range(4):
                    pairs = []
                    for half in range(2):
                        tile = j + 5 * half
                        bank = half
                        slot = half
                        wb = wdft[slot]
                        lw = pl.dma("gpsimd", wb[:].rearrange("p (k c) -> p k c", k=8),
                                    fwdv[tile], waits=W(dft_w_free.get(slot)))
                        pA = ps[:, bank * 512:bank * 512 + 512]

                        def fq_(eng, wb=wb, ch=ch, pA=pA):
                            ins = None
                            b3 = wb[:].rearrange("p (k c) -> p k c", k=8)
                            for k in range(8):
                                ins = eng.matmul(pA, b3[:, k, :],
                                                 qsb3[:, k, ch * 512:(ch + 1) * 512],
                                                 start=(k == 0), stop=(k == 7))
                            return ins

                        mm_ = pl.pe(fq_, waits=W(lw, ldq, dft_bank_free.get(bank)))
                        dft_w_free[slot] = mm_
                        pairs.append((mm_, pA))
                    (mqr, pqr), (mqi, pqi) = pairs
                    n1 = pl.act(lambda eng, pqr=pqr: eng.activation(t1[:], pqr, AF.Square),
                                waits=W(mqr, scr_free2))
                    n2 = pl.act(lambda eng, pqi=pqi: eng.activation(t2[:], pqi, AF.Square),
                                waits=W(mqi,))
                    n3 = pl.dve(lambda eng: eng.tensor_add(t1[:], t1[:], t2[:]), waits=W(n1, n2))
                    n4 = pl.act(lambda eng: eng.activation(t1[:], t1[:], AF.Sqrt), waits=W(n3,))
                    n5 = pl.dve(lambda eng: eng.tensor_scalar_add(t1[:], t1[:], 1e-8), waits=W(n4,))
                    n6 = pl.dve(lambda eng: eng.reciprocal(t1[:], t1[:]), waits=W(n5,))
                    n7 = pl.dve(lambda eng, pqr=pqr: eng.tensor_mul(t3[:], pqr, t1[:]), waits=W(n6,))
                    n8 = pl.dve(lambda eng, pqi=pqi: eng.tensor_mul(t4[:], pqi, t1[:]), waits=W(n7,))
                    dft_bank_free[0] = n7
                    dft_bank_free[1] = n8
                    c1 = pl.dve(lambda eng, j=j, ch=ch: eng.tensor_copy(
                        qn3[:, j, ch * 512:(ch + 1) * 512], t3[:]), waits=W(n7,))
                    c2 = pl.dve(lambda eng, j=j, ch=ch: eng.tensor_copy(
                        qn3[:, j + 5, ch * 512:(ch + 1) * 512], t4[:]), waits=W(n8,))
                    e1 = pl.dma("sync", cfq3[:, j, ch * 512:(ch + 1) * 512], t3[:],
                                waits=W(n7,))
                    e2 = pl.dma("sync", cfq3[:, j + 5, ch * 512:(ch + 1) * 512], t4[:],
                                waits=W(n8,))
                    scr_free2 = W(c1, c2, e1, e2)
                    qn_toks.extend([c1, c2])
                    exp_toks.extend([e1, e2])
        qn_all = W(*qn_toks)

        m_toks = []
        scr_free3 = scr_free2
        for j in range(5):
            for ch in range(4):
                sl = slice(ch * 512, (ch + 1) * 512)
                v1 = pl.dve(lambda eng, j=j, sl=sl: eng.tensor_mul(
                    t1[:], prpi3[:, j, sl], qn3[:, j, sl]),
                    waits=W(cum_all, qn_all, scr_free3))
                v2 = pl.dve(lambda eng, j=j, sl=sl: eng.tensor_mul(
                    t2[:], prpi3[:, j + 5, sl], qn3[:, j + 5, sl]), waits=W(v1,))
                v3 = pl.dve(lambda eng, j=j, sl=sl: eng.tensor_mul(
                    t3[:], prpi3[:, j + 5, sl], qn3[:, j, sl]), waits=W(v2,))
                v4 = pl.dve(lambda eng, j=j, sl=sl: eng.tensor_mul(
                    t4[:], prpi3[:, j, sl], qn3[:, j + 5, sl]), waits=W(v3,))
                w1_ = pl.dve(lambda eng, j=j, sl=sl: eng.tensor_add(
                    qn3[:, j, sl], t1[:], t2[:]), waits=W(v1, v2, v3, v4))
                w2_ = pl.dve(lambda eng, j=j, sl=sl: eng.tensor_sub(
                    qn3[:, j + 5, sl], t3[:], t4[:]), waits=W(w1_,))
                scr_free3 = w2_
                m_toks.append(w2_)
        m_all = W(*m_toks)

        winv = [st.enter_context(nc.sbuf_tensor(f"winv{i}", [128, 10 * 128], BF16))
                for i in range(2)]
        xsc = st.enter_context(nc.sbuf_tensor("xsc", [128, 8 * 512], F32))
        xs3 = xsc[:].rearrange("p (k t) -> p k t", k=8)
        out_wr = []
        ibank_free = {}
        iw_free = {}
        tmp_free = scr_free3
        for ch in range(4):
            ldx = pl.dma("sync", xs3[:, :, :], x3[:, :, ch * 512:(ch + 1) * 512],
                         waits=W(*out_wr, m_all))
            mrep = ps[:, 1024:1536]
            rrep = ps[:, 1536:2048]
            b1 = pl.pe(lambda eng, ch=ch, mrep=mrep: eng.matmul(
                mrep, sc["ones_row"][:], sc["mrow"][0:1, ch * 512:(ch + 1) * 512],
                start=True, stop=True), waits=W(ln_all, ibank_free.get("m")))
            b2 = pl.pe(lambda eng, ch=ch, rrep=rrep: eng.matmul(
                rrep, sc["ones_row"][:], sc["rrow"][0:1, ch * 512:(ch + 1) * 512],
                start=True, stop=True), waits=W(ln_all, ibank_free.get("r")))
            adds = []
            for dm in range(8):
                wb = winv[dm % 2]
                lw = pl.dma("gpsimd", wb[:].rearrange("p (k c) -> p k c", k=10),
                            invv[dm], waits=W(iw_free.get(dm % 2), m_all))
                bank = dm % 2
                pA = ps[:, bank * 512:bank * 512 + 512]

                def fi(eng, wb=wb, ch=ch, pA=pA):
                    ins = None
                    b3 = wb[:].rearrange("p (k c) -> p k c", k=10)
                    for k in range(10):
                        ins = eng.matmul(pA, b3[:, k, :], qn3[:, k, ch * 512:(ch + 1) * 512],
                                         start=(k == 0), stop=(k == 9))
                    return ins

                mm_ = pl.pe(fi, waits=W(lw, m_all, ibank_free.get(bank)))
                a1 = pl.dve(lambda eng, dm=dm, mrep=mrep: eng.tensor_sub(
                    t1[:], xs3[:, dm, :], mrep), waits=W(b1, ldx, tmp_free))
                a2 = pl.dve(lambda eng, dm=dm, rrep=rrep: eng.tensor_mul(
                    t1[:], t1[:], rrep), waits=W(b2, a1))
                a3 = pl.dve(lambda eng, dm=dm: eng.tensor_add(
                    xs3[:, dm, :], xs3[:, dm, :], t1[:]), waits=W(a2,))
                a4 = pl.dve(lambda eng, dm=dm, pA=pA: eng.tensor_add(
                    xs3[:, dm, :], xs3[:, dm, :], pA), waits=W(a3, mm_))
                tmp_free = a2
                ibank_free[bank] = a4
                iw_free[dm % 2] = mm_
                adds.append(a4)
            ibank_free["m"] = adds[-1]
            ibank_free["r"] = adds[-1]
            wr = pl.dma("sync", xo3[:, :, ch * 512:(ch + 1) * 512], xs3[:, :, :],
                        waits=W(*adds))
            out_wr = [wr]
        final = W(*out_wr, wr_tot, *exp_toks)
        pl.dve(lambda eng: eng.memset(fin_t[:], 0.0), waits=W(final))
        run_plan(nc, pl)
    return nc


def build_fftfin():
    nc = bass.Bass("TRN2", target_bir_lowering=False, debug=False)
    x = nc.dram_tensor("x", [D, TC], F32, kind="ExternalInput").ap()
    cfq = nc.dram_tensor("cfq", [2 * FP, TC], F32, kind="ExternalInput").ap()
    tstk = nc.dram_tensor("tstk", [128, 40], F32, kind="ExternalInput").ap()
    inv = nc.dram_tensor("inv", [2 * FP, D], F32, kind="ExternalInput").ap()
    xo = nc.dram_tensor("xo", [D, TC], F32, kind="ExternalOutput").ap()

    pl = Plan()
    x3 = x.rearrange("(k p) t -> p k t", p=128)
    xo3 = xo.rearrange("(k p) t -> p k t", p=128)
    invv = inv.rearrange("(k p) (m c) -> m p k c", p=128, m=8)

    with contextlib.ExitStack() as st:
        ps = st.enter_context(nc.psum_tensor("ps", [128, 2048], F32))
        fin_t = st.enter_context(nc.sbuf_tensor("fin", [1, 1], F32))
        qn = st.enter_context(nc.sbuf_tensor("qn", [128, 10 * TC], BF16))
        car = st.enter_context(nc.sbuf_tensor("car", [128, 10], F32))
        tsk = st.enter_context(nc.sbuf_tensor("tsk", [128, 40], F32))
        t1 = st.enter_context(nc.sbuf_tensor("t1", [128, 512], F32))
        t2 = st.enter_context(nc.sbuf_tensor("t2", [128, 512], F32))
        t3 = st.enter_context(nc.sbuf_tensor("t3", [128, 512], F32))
        t4 = st.enter_context(nc.sbuf_tensor("t4", [128, 512], F32))
        mb = st.enter_context(nc.sbuf_tensor("mb", [128, 10 * TC], BF16))
        winv = [st.enter_context(nc.sbuf_tensor(f"winv{i}", [128, 10 * 128], BF16))
                for i in range(2)]
        xsc = st.enter_context(nc.sbuf_tensor("xsc", [128, 8 * 512], F32))
        qn3 = qn[:].rearrange("p (j t) -> p j t", j=10)
        mb3 = mb[:].rearrange("p (j t) -> p j t", j=10)
        xs3 = xsc[:].rearrange("p (k t) -> p k t", k=8)

        ldq = pl.dma("gpsimd", qn3, cfq.rearrange("(j p) t -> p j t", p=128))
        ldt = pl.dma("sync", tsk[:], tstk)
        cr = pl.dve(lambda eng: eng.tensor_reduce(
            car[:], tsk[:].rearrange("p (j f) -> p j f", j=10), AX.X, ALU.add),
            waits=W(ldt,))
        m_toks = []
        scr = None
        for j in range(5):
            for ch in range(4):
                sl = slice(ch * 512, (ch + 1) * 512)
                v1 = pl.dve(lambda eng, j=j, sl=sl: eng.tensor_scalar_mul(
                    t1[:], qn3[:, j, sl], car[:, j:j + 1]), waits=W(cr, ldq, scr))
                v2 = pl.dve(lambda eng, j=j, sl=sl: eng.tensor_scalar_mul(
                    t2[:], qn3[:, j + 5, sl], car[:, j + 5:j + 6]), waits=W(v1,))
                v3 = pl.dve(lambda eng, j=j, sl=sl: eng.tensor_scalar_mul(
                    t3[:], qn3[:, j, sl], car[:, j + 5:j + 6]), waits=W(v2,))
                v4 = pl.dve(lambda eng, j=j, sl=sl: eng.tensor_scalar_mul(
                    t4[:], qn3[:, j + 5, sl], car[:, j:j + 1]), waits=W(v3,))
                w1_ = pl.dve(lambda eng, j=j, sl=sl: eng.tensor_add(
                    mb3[:, j, sl], t1[:], t2[:]), waits=W(v1, v2, v3, v4))
                w2_ = pl.dve(lambda eng, j=j, sl=sl: eng.tensor_sub(
                    mb3[:, j + 5, sl], t3[:], t4[:]), waits=W(w1_,))
                scr = w2_
                m_toks.append(w2_)
        m_all = W(*m_toks)

        out_wr = []
        ibank_free = {}
        iw_free = {}
        tmp_free = scr
        for ch in range(4):
            ldx = pl.dma("sync", xs3[:, :, :], x3[:, :, ch * 512:(ch + 1) * 512],
                         waits=W(*out_wr))
            adds = []
            for dm in range(8):
                wb = winv[dm % 2]
                lw = pl.dma("gpsimd", wb[:].rearrange("p (k c) -> p k c", k=10),
                            invv[dm], waits=W(iw_free.get(dm % 2)))
                bank = dm % 2
                pA = ps[:, bank * 512:bank * 512 + 512]

                def fi(eng, wb=wb, ch=ch, pA=pA):
                    ins = None
                    b3 = wb[:].rearrange("p (k c) -> p k c", k=10)
                    for k in range(10):
                        ins = eng.matmul(pA, b3[:, k, :], mb3[:, k, ch * 512:(ch + 1) * 512],
                                         start=(k == 0), stop=(k == 9))
                    return ins

                mm_ = pl.pe(fi, waits=W(lw, m_all, ibank_free.get(bank)))
                a4 = pl.dve(lambda eng, dm=dm, pA=pA: eng.tensor_add(
                    xs3[:, dm, :], xs3[:, dm, :], pA), waits=W(mm_, ldx))
                ibank_free[bank] = a4
                iw_free[dm % 2] = mm_
                adds.append(a4)
            wr = pl.dma("sync", xo3[:, :, ch * 512:(ch + 1) * 512], xs3[:, :, :],
                        waits=W(*adds))
            out_wr = [wr]
        final = W(*out_wr)
        pl.dve(lambda eng: eng.memset(fin_t[:], 0.0), waits=W(final))
        run_plan(nc, pl)
    return nc


def build_moe():
    nc = bass.Bass("TRN2", target_bir_lowering=False, debug=False)
    x = nc.dram_tensor("x", [D, TC], F32, kind="ExternalInput").ap()
    rw = nc.dram_tensor("rw", [D, 7], F32, kind="ExternalInput").ap()
    ind7 = nc.dram_tensor("ind7", [7, 7 * 128], F32, kind="ExternalInput").ap()
    eye = nc.dram_tensor("eye", [128, 128], F32, kind="ExternalInput").ap()
    shg = nc.dram_tensor("shg", [D, 4096], F32, kind="ExternalInput").ap()
    shu = nc.dram_tensor("shu", [D, 4096], F32, kind="ExternalInput").ap()
    shd = nc.dram_tensor("shd", [4096, D], F32, kind="ExternalInput").ap()
    rg = nc.dram_tensor("rg", [7, D, 4096], F32, kind="ExternalInput").ap()
    ru = nc.dram_tensor("ru", [7, D, 4096], F32, kind="ExternalInput").ap()
    rd = nc.dram_tensor("rd", [7, 4096, D], F32, kind="ExternalInput").ap()
    xo = nc.dram_tensor("xo", [D, TC], F32, kind="ExternalOutput").ap()

    pl = Plan()
    x3 = x.rearrange("(k p) t -> p k t", p=128)
    xo3 = xo.rearrange("(k p) t -> p k t", p=128)
    shgv = shg.rearrange("(k p) (m c) -> m p k c", p=128, m=32)
    shuv = shu.rearrange("(k p) (m c) -> m p k c", p=128, m=32)
    shdv = shd.rearrange("(k p) (m c) -> m p k c", p=128, m=8)
    rgv = rg.rearrange("e (k p) (m c) -> e m p k c", p=128, m=32)
    ruv = ru.rearrange("e (k p) (m c) -> e m p k c", p=128, m=32)
    rdv = rd.rearrange("e (k p) (m c) -> e m p k c", p=128, m=8)

    with contextlib.ExitStack() as st:
        ps = st.enter_context(nc.psum_tensor("ps", [128, 2048], F32))
        fin_t = st.enter_context(nc.sbuf_tensor("fin", [1, 1], F32))
        xh = st.enter_context(nc.sbuf_tensor("xh", [128, 8 * TC], BF16))
        cf = st.enter_context(nc.sbuf_tensor("cf", [7, TC], F32))
        ind_sb = st.enter_context(nc.sbuf_tensor("ind_sb", [7, 7 * 128], F32))
        eye_sb = st.enter_context(nc.sbuf_tensor("eye_sb", [128, 128], F32))
        rw_sb = st.enter_context(nc.sbuf_tensor("rw_sb", [128, 8 * 7], BF16))
        prb = st.enter_context(nc.sbuf_tensor("prb", [128, 7], F32))
        scp = st.enter_context(nc.sbuf_tensor("scp", [128, 14], F32))
        mrow1 = st.enter_context(nc.sbuf_tensor("mrow1", [128, 4], F32))
        crep = st.enter_context(nc.sbuf_tensor("crep", [128, 1024], F32))
        sA = st.enter_context(nc.sbuf_tensor("sA", [128, 512], F32))
        hh = st.enter_context(nc.sbuf_tensor("hh", [128, 32 * 1024], BF16))
        xsc = st.enter_context(nc.sbuf_tensor("xsc", [128, 8 * 1024], F32))
        wg = [st.enter_context(nc.sbuf_tensor(f"wg{i}", [128, 8 * 128], BF16))
              for i in range(4)]
        wd = [st.enter_context(nc.sbuf_tensor(f"wd{i}", [128, 32 * 128], BF16))
              for i in range(1)]
        xh3 = xh[:].rearrange("p (k t) -> p k t", k=8)
        hh3 = hh[:].rearrange("p (m t) -> p m t", m=32)
        xs4 = xsc[:].rearrange("p (k t) -> p k t", k=8)
        rw3 = rw_sb[:].rearrange("p (k c) -> p k c", k=8)

        ldc = [pl.dma("sync", ind_sb[:], ind7),
               pl.dma("sync", eye_sb[:], eye),
               pl.dma("gpsimd", rw3, rw.rearrange("(k p) c -> p k c", p=128))]
        with contextlib.ExitStack() as stln:
            sc = alloc_ln_scratch(st, nc, big_st=stln)
            with nc.psum_tensor("ps_ln", [128, 2048], F32) as ps_ln:
                cdep = emit_ln_setup(pl, sc)
                ln_toks, ln_all = emit_ln(pl, x3, xh, sc, ps_ln, cdep)
        ps2 = st.enter_context(nc.psum_tensor("ps2", [128, 1024], F32))

        cf_toks = []
        psr_free = None
        pst_free = None
        prev_tr = []
        for tt in range(16):
            psr = ps2[:, 512:512 + 7]
            def fr(eng, tt=tt, psr=psr):
                ins = None
                for k in range(8):
                    ins = eng.matmul(psr, xh3[:, k, tt * 128:(tt + 1) * 128],
                                     rw3[:, k, :], start=(k == 0), stop=(k == 7))
                return ins
            mm_ = pl.pe(fr, waits=W(*ldc, ln_all, psr_free))
            s1 = pl.dve(lambda eng: eng.tensor_reduce(mrow1[:, 0:1], psr, AX.X, ALU.max),
                        waits=W(mm_, cf_toks[-1] if cf_toks else None))
            s2 = pl.dve(lambda eng: eng.tensor_scalar(
                prb[:], psr, mrow1[:, 0:1], None, ALU.subtract), waits=W(s1,))
            psr_free = s2
            s3 = pl.act(lambda eng: eng.activation(prb[:], prb[:], AF.Exp), waits=W(s2,))
            s4 = pl.dve(lambda eng: eng.tensor_reduce(mrow1[:, 0:1], prb[:], AX.X, ALU.add),
                        waits=W(s3,))
            s5 = pl.dve(lambda eng: eng.reciprocal(mrow1[:, 0:1], mrow1[:, 0:1]), waits=W(s4,))
            s6 = pl.dve(lambda eng: eng.tensor_scalar(
                prb[:], prb[:], mrow1[:, 0:1], None, ALU.mult), waits=W(s5,))
            u1 = pl.dve(lambda eng: eng.tensor_reduce(mrow1[:, 0:1], prb[:], AX.X, ALU.max),
                        waits=W(s6,))
            u2 = pl.dve(lambda eng: eng.tensor_scalar(
                scp[:, 0:7], prb[:], mrow1[:, 0:1], None, ALU.is_ge), waits=W(u1,))
            u3 = pl.dve(lambda eng: eng.tensor_scalar(
                scp[:, 7:14], scp[:, 0:7], -1.0, 1.0, ALU.mult, ALU.add), waits=W(u2,))
            u4 = pl.dve(lambda eng: eng.tensor_mul(scp[:, 7:14], prb[:], scp[:, 7:14]),
                        waits=W(u3,))
            u5 = pl.dve(lambda eng: eng.tensor_reduce(mrow1[:, 1:2], scp[:, 7:14], AX.X, ALU.max),
                        waits=W(u4,))
            u6 = pl.dve(lambda eng: eng.tensor_scalar(
                scp[:, 7:14], scp[:, 7:14], mrow1[:, 1:2], None, ALU.is_ge), waits=W(u5,))
            u7 = pl.dve(lambda eng: eng.tensor_scalar(
                sA[:, 0:7], scp[:, 0:7], -1.0, 1.0, ALU.mult, ALU.add), waits=W(u6,))
            u8 = pl.dve(lambda eng: eng.tensor_mul(scp[:, 7:14], scp[:, 7:14], sA[:, 0:7]),
                        waits=W(u7,))
            u9 = pl.dve(lambda eng: eng.tensor_add(mrow1[:, 2:3], mrow1[:, 0:1], mrow1[:, 1:2]),
                        waits=W(u5,))
            u10 = pl.dve(lambda eng: eng.tensor_scalar_add(mrow1[:, 2:3], mrow1[:, 2:3], 1e-9),
                         waits=W(u9,))
            u11 = pl.dve(lambda eng: eng.reciprocal(mrow1[:, 2:3], mrow1[:, 2:3]), waits=W(u10,))
            u12 = pl.dve(lambda eng: eng.tensor_add(scp[:, 0:7], scp[:, 0:7], scp[:, 7:14]),
                         waits=W(u8,))
            u13 = pl.dve(lambda eng: eng.tensor_mul(prb[:], prb[:], scp[:, 0:7]), waits=W(u12,))
            u14 = pl.dve(lambda eng: eng.tensor_scalar(
                prb[:], prb[:], mrow1[:, 2:3], None, ALU.mult), waits=W(u11, u13))
            pst = ps2[0:7, 0:128]
            throttle = prev_tr[-2] if len(prev_tr) >= 2 else None
            trt = pl.pe(lambda eng, pst=pst: eng.transpose(
                pst, prb[:], eye_sb[:]), waits=W(u14, pst_free, throttle))
            ev = pl.dve(lambda eng, tt=tt, pst=pst: eng.tensor_copy(
                cf[:, tt * 128:(tt + 1) * 128], pst), waits=W(trt,))
            pst_free = ev
            prev_tr.append(ev)
            cf_toks.append(ev)
        cf_all = W(*cf_toks)

        out_wr = []
        wg_free = {}
        wd_free = {}
        bank_free = {}
        crep_free = None
        sA_free = None
        xs_busy = {}
        for half in range(2):
            h0 = half * 1024
            hh_ready = {}
            for ei in range(8):
                for m in range(32):
                    bg = wg[(2 * m) % 4]
                    bu = wg[(2 * m + 1) % 4]
                    gsrc = shgv[m] if ei == 0 else rgv[ei - 1, m]
                    usrc = shuv[m] if ei == 0 else ruv[ei - 1, m]
                    lg = pl.dma("gpsimd", bg[:].rearrange("p (k c) -> p k c", k=8),
                                gsrc, waits=W(wg_free.get((2 * m) % 4)))
                    lu = pl.dma("gpsimd", bu[:].rearrange("p (k c) -> p k c", k=8),
                                usrc, waits=W(wg_free.get((2 * m + 1) % 4)))
                    for cc in range(2):
                        t0 = h0 + cc * 512
                        bank = (2 * (m * 2 + cc)) % 4
                        bank2 = bank + 1
                        pG = ps[:, bank * 512:bank * 512 + 512]
                        pU = ps[:, bank2 * 512:bank2 * 512 + 512]

                        def fg(eng, bg=bg, t0=t0, pG=pG):
                            ins = None
                            b3 = bg[:].rearrange("p (k c) -> p k c", k=8)
                            for k in range(8):
                                ins = eng.matmul(pG, b3[:, k, :], xh3[:, k, t0:t0 + 512],
                                                 start=(k == 0), stop=(k == 7))
                            return ins

                        def fu(eng, bu=bu, t0=t0, pU=pU):
                            ins = None
                            b3 = bu[:].rearrange("p (k c) -> p k c", k=8)
                            for k in range(8):
                                ins = eng.matmul(pU, b3[:, k, :], xh3[:, k, t0:t0 + 512],
                                                 start=(k == 0), stop=(k == 7))
                            return ins

                        mg = pl.pe(fg, waits=W(lg, ln_all, bank_free.get(bank)))
                        mu = pl.pe(fu, waits=W(lu, bank_free.get(bank2)))
                        a1 = pl.act(lambda eng, pG=pG: eng.activation(
                            sA[:], pG, AF.Silu), waits=W(mg, sA_free))
                        h1 = pl.dve(lambda eng, m=m, cc=cc, pU=pU: eng.tensor_tensor(
                            hh3[:, m, cc * 512:(cc + 1) * 512], sA[:], pU, ALU.mult),
                            waits=W(a1, mu, hh_ready.get((m, cc))))
                        sA_free = h1
                        bank_free[bank] = a1
                        bank_free[bank2] = h1
                    wg_free[(2 * m) % 4] = mg
                    wg_free[(2 * m + 1) % 4] = mu
                hh_all = W(*[bank_free[b] for b in range(4) if b in bank_free])
                down_mms = []
                crts = {}
                for cc in range(2):
                    t0 = h0 + cc * 512
                    xs3 = xs4[:, :, cc * 512:(cc + 1) * 512]
                    if ei == 0:
                        ldx = pl.dma("sync", xs3, x3[:, :, t0:t0 + 512],
                                     waits=W(*out_wr, xs_busy.get(cc)))
                        xs_busy[("ld", cc)] = ldx
                    if ei > 0:
                        pC = ps2[:, (cc % 2) * 512:(cc % 2) * 512 + 512]
                        crm = pl.pe(lambda eng, ei=ei, t0=t0, pC=pC: eng.matmul(
                            pC, ind_sb[:, (ei - 1) * 128:ei * 128], cf[:, t0:t0 + 512],
                            start=True, stop=True), waits=W(cf_all, crep_free))
                        crt = pl.dve(lambda eng, cc=cc, pC=pC: eng.tensor_copy(
                            crep[:, cc * 512:(cc + 1) * 512], pC), waits=W(crm, crep_free))
                        crts[cc] = crt
                if ei > 0:
                    crep_free = W(*crts.values())
                for dm in range(8):
                    bw = wd[0]
                    lwd = pl.dma("gpsimd", bw[:].rearrange("p (k c) -> p k c", k=32),
                                 (shdv[dm] if ei == 0 else rdv[ei - 1, dm]),
                                 waits=W(wd_free.get(0)))
                    for cc in range(2):
                        t0 = h0 + cc * 512
                        xs3 = xs4[:, :, cc * 512:(cc + 1) * 512]
                        bank = (2 * dm + cc) % 4
                        pD = ps[:, bank * 512:bank * 512 + 512]

                        def fd(eng, bw=bw, cc=cc, pD=pD):
                            ins = None
                            b3 = bw[:].rearrange("p (k c) -> p k c", k=32)
                            for k in range(32):
                                ins = eng.matmul(pD, b3[:, k, :],
                                                 hh3[:, k, cc * 512:(cc + 1) * 512],
                                                 start=(k == 0), stop=(k == 31))
                            return ins

                        md = pl.pe(fd, waits=W(lwd, hh_all, bank_free.get(bank)))
                        down_mms.append(md)
                        if ei == 0:
                            ad = pl.dve(lambda eng, dm=dm, xs3=xs3, pD=pD: eng.tensor_add(
                                xs3[:, dm, :], xs3[:, dm, :], pD),
                                waits=W(md, xs_busy.get(("ld", cc))))
                        else:
                            t_ = pl.dve(lambda eng, cc=cc, pD=pD: eng.tensor_mul(
                                sA[:], pD, crep[:, cc * 512:(cc + 1) * 512]),
                                waits=W(md, crts.get(cc), sA_free))
                            ad = pl.dve(lambda eng, dm=dm, xs3=xs3: eng.tensor_add(
                                xs3[:, dm, :], xs3[:, dm, :], sA[:]), waits=W(t_,))
                            sA_free = ad
                        bank_free[bank] = ad
                        xs_busy[(cc, dm)] = ad
                    wd_free[0] = W(*down_mms[-2:])
                if ei == 7:
                    for cc in range(2):
                        t0 = h0 + cc * 512
                        xs3 = xs4[:, :, cc * 512:(cc + 1) * 512]
                        wr = pl.dma("sync", xo3[:, :, t0:t0 + 512], xs3,
                                    waits=W(*[xs_busy[(cc, dm)] for dm in range(8)]))
                        out_wr.append(wr)
                        xs_busy[cc] = wr
                guard = W(*down_mms)
                for m in range(32):
                    for cc in range(2):
                        hh_ready[(m, cc)] = guard
        final = W(*out_wr)
        pl.dve(lambda eng: eng.memset(fin_t[:], 0.0), waits=W(final))
        run_plan(nc, pl)
    return nc


def build_mpre1():
    nc = bass.Bass("TRN2", target_bir_lowering=False, debug=False)
    x = nc.dram_tensor("x", [D, TC], F32, kind="ExternalInput").ap()
    ipw = nc.dram_tensor("ipw", [D, 2 * DI], F32, kind="ExternalInput").ap()
    xz = nc.dram_tensor("xz", [2 * DI, TC], F32, kind="ExternalOutput").ap()

    pl = Plan()
    x3 = x.rearrange("(k p) t -> p k t", p=128)
    xz3 = xz.rearrange("(m p) t -> p m t", p=128)
    ipv = ipw.rearrange("(k p) (m c) -> m p k c", p=128, m=32)

    with contextlib.ExitStack() as st:
        ps = st.enter_context(nc.psum_tensor("ps", [128, 2048], F32))
        fin_t = st.enter_context(nc.sbuf_tensor("fin", [1, 1], F32))
        xh = st.enter_context(nc.sbuf_tensor("xh", [128, 8 * TC], BF16))
        stg = [st.enter_context(nc.sbuf_tensor(f"stg{i}", [128, 512], F32))
               for i in range(4)]
        wpr = [st.enter_context(nc.sbuf_tensor(f"wpr{i}", [128, 8 * 128], BF16))
               for i in range(3)]
        with contextlib.ExitStack() as stln:
            sc = alloc_ln_scratch(st, nc, big_st=stln)
            with nc.psum_tensor("ps_ln", [128, 2048], F32) as ps_ln:
                cdep = emit_ln_setup(pl, sc)
                ln_toks, ln_all = emit_ln(pl, x3, xh, sc, ps_ln, cdep)
        xh3 = xh[:].rearrange("p (k t) -> p k t", k=8)

        stg_free = {}
        bank_free = {}
        wfree = {}
        wr_toks = []
        for m in range(32):
            wb = wpr[m % 3]
            lw = pl.dma("gpsimd", wb[:].rearrange("p (k c) -> p k c", k=8),
                        ipv[m], waits=W(wfree.get(m % 3)))
            for ch in range(4):
                bank = (m * 4 + ch) % 4
                pA = ps[:, bank * 512:bank * 512 + 512]

                def fp(eng, ch=ch, wb=wb, pA=pA):
                    ins = None
                    b3 = wb[:].rearrange("p (k c) -> p k c", k=8)
                    for k in range(8):
                        ins = eng.matmul(pA, b3[:, k, :], xh3[:, k, ch * 512:(ch + 1) * 512],
                                         start=(k == 0), stop=(k == 7))
                    return ins

                mm_ = pl.pe(fp, waits=W(lw, ln_all, bank_free.get(bank)))
                sti = (m * 4 + ch) % 4
                ev = pl.dve(lambda eng, sti=sti, pA=pA: eng.tensor_copy(
                    stg[sti][:], pA), waits=W(mm_, stg_free.get(sti)))
                dt_ = pl.dma("sync", xz3[:, m, ch * 512:(ch + 1) * 512], stg[sti][:],
                             waits=W(ev,))
                bank_free[bank] = ev
                stg_free[sti] = dt_
                wfree[m % 3] = mm_
                wr_toks.append(dt_)
        final = W(*wr_toks)
        pl.dve(lambda eng: eng.memset(fin_t[:], 0.0), waits=W(final))
        run_plan(nc, pl)
    return nc


def build_mpre2():
    nc = bass.Bass("TRN2", target_bir_lowering=False, debug=False)
    xi = nc.dram_tensor("xi", [DI, TC], F32, kind="ExternalInput").ap()
    tailp = nc.dram_tensor("tailp", [DI, 3], F32, kind="ExternalInput").ap()
    cw = nc.dram_tensor("cw", [128, 16 * 4], F32, kind="ExternalInput").ap()
    cb = nc.dram_tensor("cb", [128, 16], F32, kind="ExternalInput").ap()
    xpw = nc.dram_tensor("xpw", [DI, 96], F32, kind="ExternalInput").ap()
    dtw = nc.dram_tensor("dtw", [64, DI], F32, kind="ExternalInput").ap()
    dtb = nc.dram_tensor("dtb", [128, 16], F32, kind="ExternalInput").ap()
    xc = nc.dram_tensor("xc", [DI, TC], F32, kind="ExternalOutput").ap()
    dto = nc.dram_tensor("dto", [DI, TC], F32, kind="ExternalOutput").ap()
    bco = nc.dram_tensor("bco", [32, TC], F32, kind="ExternalOutput").ap()

    pl = Plan()
    xi3 = xi.rearrange("(a p) t -> p a t", p=128)
    xc3 = xc.rearrange("(a p) t -> p a t", p=128)
    dto3 = dto.rearrange("(a p) t -> p a t", p=128)
    tl3 = tailp.rearrange("(a p) t -> p a t", p=128)
    xpv = xpw.rearrange("(k p) c -> p k c", p=128)

    with contextlib.ExitStack() as st:
        ps = st.enter_context(nc.psum_tensor("ps", [128, 2048], F32))
        fin_t = st.enter_context(nc.sbuf_tensor("fin", [1, 1], F32))
        cw_sb = st.enter_context(nc.sbuf_tensor("cw_sb", [128, 64], F32))
        cb_sb = st.enter_context(nc.sbuf_tensor("cb_sb", [128, 16], F32))
        dtb_sb = st.enter_context(nc.sbuf_tensor("dtb_sb", [128, 16], F32))
        xpw_sb = st.enter_context(nc.sbuf_tensor("xpw_sb", [128, 16 * 96], BF16))
        dtw_sb = st.enter_context(nc.sbuf_tensor("dtw_sb", [64, DI], BF16))
        xcb = st.enter_context(nc.sbuf_tensor("xcb", [128, 16 * TC], BF16))
        dtr = st.enter_context(nc.sbuf_tensor("dtr", [64, TC], BF16))
        xip = st.enter_context(nc.sbuf_tensor("xip", [128, 515], F32))
        t1 = st.enter_context(nc.sbuf_tensor("t1", [128, 512], F32))
        t2 = st.enter_context(nc.sbuf_tensor("t2", [128, 512], F32))
        t3 = st.enter_context(nc.sbuf_tensor("t3", [128, 512], F32))
        xcb3 = xcb[:].rearrange("p (a t) -> p a t", a=16)
        cw3 = cw_sb[:].rearrange("p (a j) -> p a j", a=16)

        lds = [pl.dma("sync", cw_sb[:], cw), pl.dma("sync", cb_sb[:], cb),
               pl.dma("sync", dtb_sb[:], dtb),
               pl.dma("gpsimd", xpw_sb[:].rearrange("p (k c) -> p k c", k=16), xpv),
               pl.dma("gpsimd", dtw_sb[:], dtw)]
        ldw = W(*lds)

        xc_toks = []
        xip_free = None
        t_free = None
        for a in range(16):
            for ch in range(4):
                t0 = ch * 512
                if ch == 0:
                    l1 = pl.dma("sync", xip[:, 0:3], tl3[:, a, :], waits=W(xip_free))
                    l2 = pl.dma("sync", xip[:, 3:515], xi3[:, a, 0:512], waits=W(xip_free))
                    lx = W(l1, l2)
                else:
                    lx = W(pl.dma("sync", xip[:, 0:515], xi3[:, a, t0 - 3:t0 + 512],
                                  waits=W(xip_free)))
                c0 = pl.dve(lambda eng, a=a: eng.tensor_scalar(
                    t1[:], xip[:, 0:512], cw3[:, a, 0:1], None, ALU.mult),
                    waits=W(lx, ldw, t_free))
                c1 = pl.dve(lambda eng, a=a: eng.scalar_tensor_tensor(
                    t1[:], xip[:, 1:513], cw3[:, a, 1:2], t1[:], ALU.mult, ALU.add),
                    waits=W(c0,))
                c2 = pl.dve(lambda eng, a=a: eng.scalar_tensor_tensor(
                    t1[:], xip[:, 2:514], cw3[:, a, 2:3], t1[:], ALU.mult, ALU.add),
                    waits=W(c1,))
                c3 = pl.dve(lambda eng, a=a: eng.scalar_tensor_tensor(
                    t1[:], xip[:, 3:515], cw3[:, a, 3:4], t1[:], ALU.mult, ALU.add),
                    waits=W(c2,))
                xip_free = c3
                s_ = pl.act(lambda eng, a=a: eng.activation(
                    t2[:], t1[:], AF.Silu, bias=cb_sb[:, a:a + 1]), waits=W(c3,))
                cp = pl.dve(lambda eng, a=a, ch=ch: eng.tensor_copy(
                    xcb3[:, a, ch * 512:(ch + 1) * 512], t2[:]), waits=W(s_,))
                wr = pl.dma("sync", xc3[:, a, t0:t0 + 512], t2[:], waits=W(s_,))
                t_free = W(cp, wr)
                xc_toks.append(W(cp, wr))
        xc_all = W(*xc_toks)

        bc_toks = []
        dtr_toks = []
        psx_free = None
        for ch in range(4):
            pA = ps[0:96, 1536:1536 + 512]

            def fx(eng, ch=ch, pA=pA):
                ins = None
                x3_ = xpw_sb[:].rearrange("p (k c) -> p k c", k=16)
                for k in range(16):
                    ins = eng.matmul(pA, x3_[:, k, :], xcb3[:, k, ch * 512:(ch + 1) * 512],
                                     start=(k == 0), stop=(k == 15))
                return ins

            mm_ = pl.pe(fx, waits=W(ldw, xc_all, psx_free))
            e1 = pl.dve(lambda eng, ch=ch, pA=pA: eng.tensor_copy(
                dtr[:, ch * 512:(ch + 1) * 512], pA[0:64, :]), waits=W(mm_,))
            e1b = pl.dve(lambda eng, ch=ch, pA=pA: eng.tensor_copy(
                t3[0:32, :], pA[64:96, :]), waits=W(mm_, bc_toks[-1] if bc_toks else None))
            e2 = pl.dma("sync", bco[:, ch * 512:(ch + 1) * 512], t3[0:32, :],
                        waits=W(e1b,))
            bc_toks.append(e2)
            dtr_toks.append(e1)
            psx_free = W(e1, e1b)
        dtr_all = W(*dtr_toks)

        dt_toks = []
        bank_free = {}
        tf = W(*bc_toks)
        for m in range(16):
            for ch in range(4):
                bank = (m * 4 + ch) % 3
                pA = ps[:, bank * 512:bank * 512 + 512]
                mm_ = pl.pe(lambda eng, m=m, ch=ch, pA=pA: eng.matmul(
                    pA, dtw_sb[:, m * 128:(m + 1) * 128],
                    dtr[:, ch * 512:(ch + 1) * 512], start=True, stop=True),
                    waits=W(dtr_all, ldw, bank_free.get(bank)))
                a0 = pl.act(lambda eng, m=m, pA=pA: eng.activation(
                    t1[:], pA, AF.Identity, bias=dtb_sb[:, m:m + 1]),
                    waits=W(mm_, tf))
                a1 = pl.act(lambda eng: eng.activation(t2[:], t1[:], AF.Abs),
                            waits=W(a0,))
                a2 = pl.act(lambda eng: eng.activation(t2[:], t2[:], AF.Exp, scale=-1.0),
                            waits=W(a1,))
                a3 = pl.act(lambda eng: eng.activation(t2[:], t2[:], AF.Ln, bias=1.0),
                            waits=W(a2,))
                a4 = pl.dve(lambda eng: eng.tensor_relu(t3[:], t1[:]), waits=W(a1,))
                a5 = pl.dve(lambda eng: eng.tensor_add(t2[:], t2[:], t3[:]), waits=W(a3, a4))
                wr = pl.dma("sync", dto3[:, m, ch * 512:(ch + 1) * 512], t2[:],
                            waits=W(a5,))
                tf = W(a5, wr)
                bank_free[bank] = a0
                dt_toks.append(wr)
        final = W(*dt_toks, *bc_toks, xc_all)
        pl.dve(lambda eng: eng.memset(fin_t[:], 0.0), waits=W(final))
        run_plan(nc, pl)
    return nc


def build_mscan():
    TT_ = T
    nc = bass.Bass("TRN2", target_bir_lowering=False, debug=False)
    dt_ = nc.dram_tensor("dt", [512, TT_], F32, kind="ExternalInput").ap()
    xcs = nc.dram_tensor("xcs", [512, TT_], F32, kind="ExternalInput").ap()
    Bv = nc.dram_tensor("Bv", [16, TT_], F32, kind="ExternalInput").ap()
    Cv = nc.dram_tensor("Cv", [16, TT_], F32, kind="ExternalInput").ap()
    Al = nc.dram_tensor("Al", [128, 64], F32, kind="ExternalInput").ap()
    blk8 = nc.dram_tensor("blk8", [128, 8], F32, kind="ExternalInput").ap()
    yo = nc.dram_tensor("yo", [512, TT_], F32, kind="ExternalOutput").ap()

    pl = Plan()
    NCH = 2048
    dt4 = dt_.rearrange("(a d8) t -> a d8 t", d8=8)
    xc4 = xcs.rearrange("(a d8) t -> a d8 t", d8=8)
    yo4 = yo.rearrange("(a d8) t -> a d8 t", d8=8)

    with contextlib.ExitStack() as st:
        ps = st.enter_context(nc.psum_tensor("ps", [128, 2048], F32))
        fin_t = st.enter_context(nc.sbuf_tensor("fin", [1, 1], F32))
        Al_sb = st.enter_context(nc.sbuf_tensor("Al_sb", [128, 64], F32))
        blk_sb = st.enter_context(nc.sbuf_tensor("blk_sb", [128, 8], F32))
        dtr = st.enter_context(nc.sbuf_tensor("dtr", [128, NCH], F32))
        xcr = st.enter_context(nc.sbuf_tensor("xcr", [128, NCH], F32))
        Br = st.enter_context(nc.sbuf_tensor("Br", [128, NCH], F32))
        Cr = st.enter_context(nc.sbuf_tensor("Cr", [128, NCH], F32))
        dA = st.enter_context(nc.sbuf_tensor("dA", [128, NCH], F32))
        dBu = st.enter_context(nc.sbuf_tensor("dBu", [128, NCH], F32))
        hb = st.enter_context(nc.sbuf_tensor("hb", [128, NCH], F32))
        hc = st.enter_context(nc.sbuf_tensor("hc", [128, 1], F32))
        ysb = st.enter_context(nc.sbuf_tensor("ysb", [128, NCH], F32))

        lA0 = pl.dma("sync", Al_sb[:], Al)
        lA1 = pl.act(lambda eng: eng.activation(Al_sb[:], Al_sb[:], AF.Exp), waits=W(lA0,))
        lA = pl.dve(lambda eng: eng.tensor_scalar_mul(Al_sb[:], Al_sb[:], -1.0), waits=W(lA1,))
        lB = pl.dma("sync", blk_sb[:], blk8)
        y_wr = []
        for a in range(64):
            carry_tok = None
            rd_free = None
            for ci in range(TT_ // NCH):
                t0 = ci * NCH
                l1 = pl.dma("sync",
                            dtr[:],
                            dt4[a][:, None, t0:t0 + NCH].broadcast_to([8, 16, NCH]),
                            waits=W(rd_free))
                l2 = pl.dma("sync",
                            xcr[:],
                            xc4[a][:, None, t0:t0 + NCH].broadcast_to([8, 16, NCH]),
                            waits=W(rd_free))
                l3 = pl.dma("sync",
                            Br[:],
                            Bv[None, :, t0:t0 + NCH].broadcast_to([8, 16, NCH]),
                            waits=W(rd_free))
                l4 = pl.dma("sync",
                            Cr[:],
                            Cv[None, :, t0:t0 + NCH].broadcast_to([8, 16, NCH]),
                            waits=W(rd_free))
                e1 = pl.act(lambda eng, a=a: eng.activation(
                    dA[:], dtr[:], AF.Exp, scale=Al_sb[:, a:a + 1]),
                    waits=W(l1, lA, carry_tok))
                e2 = pl.dve(lambda eng: eng.tensor_mul(dBu[:], dtr[:], xcr[:]),
                            waits=W(l1, l2))
                e3 = pl.dve(lambda eng: eng.tensor_mul(dBu[:], dBu[:], Br[:]),
                            waits=W(e2, l3))
                if ci == 0:
                    sck = pl.dve(lambda eng: eng.tensor_tensor_scan(
                        hb[:], dA[:], dBu[:], 0.0, ALU.mult, ALU.add),
                        waits=W(e1, e3))
                else:
                    sck = pl.dve(lambda eng: eng.tensor_tensor_scan(
                        hb[:], dA[:], dBu[:], hc[:], ALU.mult, ALU.add),
                        waits=W(e1, e3, carry_tok))
                carry_tok = pl.dve(lambda eng: eng.tensor_copy(hc[:], hb[:, NCH - 1:NCH]),
                                   waits=W(sck,))
                m1 = pl.dve(lambda eng: eng.tensor_mul(ysb[:], hb[:], Cr[:]),
                            waits=W(sck, l4))
                mm_toks = []
                for q in range(NCH // 512):
                    pY = ps[0:8, (q % 4) * 512:(q % 4) * 512 + 512]
                    mmy = pl.pe(lambda eng, q=q, pY=pY: eng.matmul(
                        pY, blk_sb[:], ysb[:, q * 512:(q + 1) * 512],
                        start=True, stop=True), waits=W(m1, lB, y_wr[-1] if y_wr else None))
                    ev = pl.dve(lambda eng, q=q, pY=pY: eng.tensor_copy(
                        ysb[0:8, q * 512:(q + 1) * 512], pY), waits=W(mmy,))
                    mm_toks.append(ev)
                wr = pl.dma("sync", yo4[a][:, t0:t0 + NCH], ysb[0:8, :],
                            waits=W(*mm_toks))
                y_wr.append(wr)
                rd_free = W(e1, e3, m1, *mm_toks)
        final = W(*y_wr)
        pl.dve(lambda eng: eng.memset(fin_t[:], 0.0), waits=W(final))
        run_plan(nc, pl)
    return nc


def build_mpost():
    nc = bass.Bass("TRN2", target_bir_lowering=False, debug=False)
    x = nc.dram_tensor("x", [D, TC], F32, kind="ExternalInput").ap()
    y = nc.dram_tensor("y", [DI, TC], F32, kind="ExternalInput").ap()
    xcin = nc.dram_tensor("xcin", [DI, TC], F32, kind="ExternalInput").ap()
    zf = nc.dram_tensor("zf", [DI, TC], F32, kind="ExternalInput").ap()
    opw = nc.dram_tensor("opw", [DI, D], F32, kind="ExternalInput").ap()
    Dv = nc.dram_tensor("Dv", [128, 16], F32, kind="ExternalInput").ap()
    lng = nc.dram_tensor("lng", [128, 8], F32, kind="ExternalInput").ap()
    lnb = nc.dram_tensor("lnb", [128, 8], F32, kind="ExternalInput").ap()
    xo = nc.dram_tensor("xo", [D, TC], F32, kind="ExternalOutput").ap()
    udr = nc.dram_tensor("udr", [D, TC], F32, kind="Internal").ap()

    pl = Plan()
    x3 = x.rearrange("(k p) t -> p k t", p=128)
    xo3 = xo.rearrange("(k p) t -> p k t", p=128)
    y3 = y.rearrange("(a p) t -> p a t", p=128)
    xc3v = xcin.rearrange("(a p) t -> p a t", p=128)
    z3 = zf.rearrange("(a p) t -> p a t", p=128)
    u3 = udr.rearrange("(k p) t -> p k t", p=128)
    opv = opw.rearrange("(k p) (m c) -> m p k c", p=128, m=8)

    with contextlib.ExitStack() as st:
        ps = st.enter_context(nc.psum_tensor("ps", [128, 2048], F32))
        fin_t = st.enter_context(nc.sbuf_tensor("fin", [1, 1], F32))
        Dsb = st.enter_context(nc.sbuf_tensor("Dsb", [128, 16], F32))
        g_sb = st.enter_context(nc.sbuf_tensor("g_sb", [128, 8], F32))
        b_sb = st.enter_context(nc.sbuf_tensor("b_sb", [128, 8], F32))
        ymb = st.enter_context(nc.sbuf_tensor("ymb", [128, 16 * TC], BF16))
        t1 = st.enter_context(nc.sbuf_tensor("t1", [128, 512], F32))
        t2 = st.enter_context(nc.sbuf_tensor("t2", [128, 512], F32))
        t3 = st.enter_context(nc.sbuf_tensor("t3", [128, 512], F32))
        xsc = st.enter_context(nc.sbuf_tensor("xsc", [128, 8 * 512], F32))
        wpr = [st.enter_context(nc.sbuf_tensor(f"wpr{i}", [128, 16 * 128], BF16))
               for i in range(2)]
        sc = alloc_ln_scratch(st, nc)
        ymb3 = ymb[:].rearrange("p (a t) -> p a t", a=16)
        xs3 = xsc[:].rearrange("p (k t) -> p k t", k=8)

        lds = [pl.dma("sync", Dsb[:], Dv), pl.dma("sync", g_sb[:], lng),
               pl.dma("sync", b_sb[:], lnb)]
        cdep = emit_ln_setup(pl, sc)

        with contextlib.ExitStack() as stln:
            xh_d = stln.enter_context(nc.sbuf_tensor("xh_d", [128, 8 * TC], BF16))
            sc2 = dict(sc)
            sc2["xc"] = stln.enter_context(nc.sbuf_tensor("ln_xc2", [128, 8 * 512], F32))
            sc2["sqa"] = stln.enter_context(nc.sbuf_tensor("ln_sqa2", [128, 512], F32))
            sc2["sqb"] = stln.enter_context(nc.sbuf_tensor("ln_sqb2", [128, 512], F32))
            sc2["sub"] = stln.enter_context(nc.sbuf_tensor("ln_sub2", [128, 512], F32))
            with nc.psum_tensor("ps_ln", [128, 2048], F32) as ps_ln:
                ln_toks, ln_all = emit_ln(pl, x3, xh_d, sc2, ps_ln, cdep)
        mrow7 = sc["mrow"]
        rrow7 = sc["rrow"]

        ym_toks = []
        tf = None
        for a in range(16):
            for ch in range(4):
                sl = slice(ch * 512, (ch + 1) * 512)
                l1 = pl.dma("sync", t1[:], y3[:, a, sl], waits=W(tf))
                l2 = pl.dma("sync", t2[:], xc3v[:, a, sl], waits=W(tf))
                l3 = pl.dma("sync", t3[:], z3[:, a, sl], waits=W(tf))
                c1 = pl.dve(lambda eng, a=a: eng.scalar_tensor_tensor(
                    t1[:], t2[:], Dsb[:, a:a + 1], t1[:], ALU.mult, ALU.add),
                    waits=W(l1, l2, *lds))
                c2 = pl.act(lambda eng: eng.activation(t3[:], t3[:], AF.Silu),
                            waits=W(l3,))
                c3 = pl.dve(lambda eng, a=a, sl=sl: eng.tensor_tensor(
                    ymb3[:, a, sl], t1[:], t3[:], ALU.mult), waits=W(c1, c2))
                tf = c3
                ym_toks.append(c3)
        ym_all = W(*ym_toks)

        u_toks = []
        tmp_free = tf
        rep_free = []
        for ch in range(4):
            mrep = ps[:, 1024:1536]
            rrep = ps[:, 1536:2048]
            b1 = pl.pe(lambda eng, ch=ch, mrep=mrep: eng.matmul(
                mrep, sc["ones_row"][:], mrow7[0:1, ch * 512:(ch + 1) * 512],
                start=True, stop=True), waits=W(ln_all, *rep_free))
            b2 = pl.pe(lambda eng, ch=ch, rrep=rrep: eng.matmul(
                rrep, sc["ones_row"][:], rrow7[0:1, ch * 512:(ch + 1) * 512],
                start=True, stop=True), waits=W(ln_all, *rep_free))
            rep_free = []
            ldx = pl.dma("sync", xs3[:, :, :], x3[:, :, ch * 512:(ch + 1) * 512],
                         waits=W(*u_toks[-8:]) if u_toks else ())
            for m in range(8):
                a1 = pl.dve(lambda eng, m=m, mrep=mrep: eng.tensor_sub(
                    t1[:], xs3[:, m, :], mrep), waits=W(b1, ldx, tmp_free))
                a2 = pl.dve(lambda eng, rrep=rrep: eng.tensor_mul(t1[:], t1[:], rrep),
                            waits=W(b2, a1))
                wr = pl.dma("sync", u3[:, m, ch * 512:(ch + 1) * 512], t1[:],
                            waits=W(a2,))
                tmp_free = wr
                rep_free.append(a2)
                u_toks.append(wr)
        xhat_all = W(*u_toks)
        wfree = {}
        bank_free = {}
        u2_toks = []
        t2_free = None
        for m in range(8):
            wb = wpr[m % 2]
            lw = pl.dma("gpsimd", wb[:].rearrange("p (k c) -> p k c", k=16),
                        opv[m], waits=W(wfree.get(m % 2)))
            for ch in range(4):
                bank = (m * 4 + ch) % 2
                pA = ps[:, bank * 512:bank * 512 + 512]

                def fo(eng, wb=wb, ch=ch, pA=pA):
                    ins = None
                    b3 = wb[:].rearrange("p (k c) -> p k c", k=16)
                    for k in range(16):
                        ins = eng.matmul(pA, b3[:, k, :], ymb3[:, k, ch * 512:(ch + 1) * 512],
                                         start=(k == 0), stop=(k == 15))
                    return ins

                mm_ = pl.pe(fo, waits=W(lw, ym_all, bank_free.get(bank)))
                lu = pl.dma("sync", t2[:], u3[:, m, ch * 512:(ch + 1) * 512],
                            waits=W(xhat_all, t2_free, ym_all))
                a3 = pl.dve(lambda eng, pA=pA: eng.tensor_add(t2[:], t2[:], pA),
                            waits=W(mm_, lu))
                wr = pl.dma("sync", u3[:, m, ch * 512:(ch + 1) * 512], t2[:],
                            waits=W(a3,))
                t2_free = wr
                bank_free[bank] = a3
                u2_toks.append(wr)
            wfree[m % 2] = W(*u2_toks[-4:])
        u_all = W(*u2_toks)

        xho = st.enter_context(nc.sbuf_tensor("xho", [128, 8 * TC], BF16))
        with contextlib.ExitStack() as stln2:
            sc3 = dict(sc)
            sc3["xc"] = stln2.enter_context(nc.sbuf_tensor("ln_xc3", [128, 8 * 512], F32))
            sc3["sqa"] = stln2.enter_context(nc.sbuf_tensor("ln_sqa3", [128, 512], F32))
            sc3["sqb"] = stln2.enter_context(nc.sbuf_tensor("ln_sqb3", [128, 512], F32))
            sc3["sub"] = stln2.enter_context(nc.sbuf_tensor("ln_sub3", [128, 512], F32))
            with nc.psum_tensor("ps_ln2", [128, 2048], F32) as ps_ln2:
                ln2_toks, ln2_all = emit_ln(pl, u3, xho, sc3, ps_ln2, W(cdep, u_all),
                                            gb=(g_sb[:], b_sb[:]))
        xho3 = xho[:].rearrange("p (k t) -> p k t", k=8)
        out_wr = []
        for ch in range(4):
            ldx = pl.dma("sync", xs3[:, :, :], x3[:, :, ch * 512:(ch + 1) * 512],
                         waits=W(*out_wr, u_all))
            adds = []
            tfq = None
            for m in range(8):
                cpq = pl.dve(lambda eng, m=m, ch=ch: eng.tensor_copy(
                    t1[:], xho3[:, m, ch * 512:(ch + 1) * 512]), waits=W(ln2_all, tfq))
                ad = pl.dve(lambda eng, m=m: eng.tensor_add(
                    xs3[:, m, :], xs3[:, m, :], t1[:]), waits=W(ldx, cpq))
                tfq = ad
                adds.append(ad)
            wr = pl.dma("sync", xo3[:, :, ch * 512:(ch + 1) * 512], xs3[:, :, :],
                        waits=W(*adds))
            out_wr = [wr]
        final = W(*out_wr)
        pl.dve(lambda eng: eng.memset(fin_t[:], 0.0), waits=W(final))
        run_plan(nc, pl)
    return nc


def ind7_mat():
    if "ind7" not in _CONST_CACHE:
        m = np.zeros((7, 7 * 128), np.float32)
        for e in range(7):
            m[e, e * 128:(e + 1) * 128] = 1.0
        _CONST_CACHE["ind7"] = m
    return _CONST_CACHE["ind7"]


def blk8_mat():
    if "blk8" not in _CONST_CACHE:
        m = np.zeros((128, 8), np.float32)
        for d8 in range(8):
            m[d8 * 16:(d8 + 1) * 16, d8] = 1.0
        _CONST_CACHE["blk8"] = m
    return _CONST_CACHE["blk8"]


_NC_CACHE = {}


def get_nc(name):
    if name not in _NC_CACHE:
        _NC_CACHE[name] = globals()["build_" + name]()
    return _NC_CACHE[name]


def _run(name, in_maps):
    nc = get_nc(name)
    return run_bass_kernel_spmd(nc, in_maps, core_ids=list(range(NCORES)))


def kernel(**inputs):
    p = {k: np.ascontiguousarray(np.asarray(v, np.float32)) for k, v in inputs.items()}
    x = p["L0_attn_a_x"]
    CT = np.ascontiguousarray

    def shards(xfull):
        out = []
        for c in range(NCORES):
            b, qq = c // 4, c % 4
            out.append(CT(xfull[b].T[:, qq * TC:(qq + 1) * TC]))
        return out

    def outs_to_full(res, key="xo"):
        xf = np.empty((B, T, D), np.float32)
        for c in range(NCORES):
            b, qq = c // 4, c % 4
            xf[b, qq * TC:(qq + 1) * TC] = res.results[c][key].T
        return xf

    xs = shards(x)

    aa = {
        "wq": p["L0_attn_a_Wq"], "bq": pm(p["L0_attn_a_bq"]),
        "wd": p["L0_attn_a_Wdown"], "bd": pm(p["L0_attn_a_bdown"]),
        "wu": p["L0_attn_a_Wup"], "bu": pm(p["L0_attn_a_bup"]),
        "wo": p["L0_attn_a_Wout"], "bo": pm(p["L0_attn_a_bout"]),
        "eye": eye128(),
    }
    r = _run("attn_a", [{**aa, "x": xs[c]} for c in range(NCORES)])
    xs = [r.results[c]["xo"] for c in range(NCORES)]

    f0 = {
        "w1": CT(p["L0_ffn_w1_w"].T), "s1": pm(p["L0_ffn_w1_s"]),
        "w2": CT(p["L0_ffn_w2_w"].T), "s2": pm(p["L0_ffn_w2_s"]),
        "h1": CT(p["L0_ffn_h1_w"].T), "sh1": pm(p["L0_ffn_h1_s"]),
        "h2": CT(p["L0_ffn_h2_w"].T), "sh2": pm(p["L0_ffn_h2_s"]),
        "w3": CT(p["L0_ffn_w3_w"].T), "s3": pm(p["L0_ffn_w3_s"]),
    }
    r = _run("ffn0", [{**f0, "x": xs[c]} for c in range(NCORES)])
    xs = [r.results[c]["xo"] for c in range(NCORES)]

    ab1 = {
        "wqb": p["L0_attn_b_W_q"], "wkb": p["L0_attn_b_W_k"],
        "wvb": p["L0_attn_b_W_v"], "mask": tril_mask(),
    }
    r1 = _run("ablocal", [{**ab1, "x": xs[c]} for c in range(NCORES)])
    in2 = []
    for c in range(NCORES):
        b, qq = c // 4, c % 4
        Sst = np.zeros((64, 192), np.float32)
        zst = np.zeros((64, 3), np.float32)
        for j in range(qq):
            Sst[:, j * 64:(j + 1) * 64] = r1.results[4 * b + j]["So"]
            zst[:, j:j + 1] = r1.results[4 * b + j]["zo"]
        in2.append({
            "x": xs[c], "ou": r1.results[c]["ou"], "denr": r1.results[c]["deno"],
            "pq": r1.results[c]["pqo"], "Sstk": Sst, "zstk": zst,
            "wob": p["L0_attn_b_W_o"],
        })
    r = _run("abfin", in2)
    xs = [r.results[c]["xo"] for c in range(NCORES)]

    r = _run("ffn0", [{**f0, "x": xs[c]} for c in range(NCORES)])
    xs = [r.results[c]["xo"] for c in range(NCORES)]

    Fwd, Inv = dft_mats()
    f5 = {"wq5": CT(p["L1_attn_a_Wq"].T), "wk5": CT(p["L1_attn_a_Wk"].T),
          "wv5": CT(p["L1_attn_a_Wv"].T), "fwd": Fwd, "inv": Inv}
    r1 = _run("fftlocal", [{**f5, "x": xs[c]} for c in range(NCORES)])
    in2 = []
    for c in range(NCORES):
        b, qq = c // 4, c % 4
        tst = np.zeros((128, 10, 4), np.float32)
        for j in range(qq):
            tst[:, :, j] = r1.results[4 * b + j]["tot"]
        in2.append({"x": r1.results[c]["xo"], "cfq": r1.results[c]["cfq"],
                    "tstk": CT(tst.reshape(128, 40)), "inv": Inv})
    r = _run("fftfin", in2)
    xs = [r.results[c]["xo"] for c in range(NCORES)]

    mo = {
        "rw": CT(p["L1_ffn_router_w"].T), "ind7": ind7_mat(), "eye": eye128(),
        "shg": CT(p["L1_ffn_shared_gate"].T), "shu": CT(p["L1_ffn_shared_up"].T),
        "shd": CT(p["L1_ffn_shared_down"].T),
        "rg": CT(np.transpose(p["L1_ffn_routed_gate"], (0, 2, 1))),
        "ru": CT(np.transpose(p["L1_ffn_routed_up"], (0, 2, 1))),
        "rd": CT(np.transpose(p["L1_ffn_routed_down"], (0, 2, 1))),
    }
    r = _run("moe", [{**mo, "x": xs[c]} for c in range(NCORES)])
    xs = [r.results[c]["xo"] for c in range(NCORES)]

    r1 = _run("mpre1", [{"x": xs[c], "ipw": CT(p["L1_attn_b_in_proj_w"].T)}
                        for c in range(NCORES)])
    cw_pm = CT(p["L1_attn_b_conv_w"][:, 0, :].reshape(16, 128, 4)
               .transpose(1, 0, 2).reshape(128, 64))
    in2 = []
    for c in range(NCORES):
        b, qq = c // 4, c % 4
        tailp = (np.zeros((DI, 3), np.float32) if qq == 0
                 else CT(r1.results[c - 1]["xz"][0:DI, -3:]))
        in2.append({
            "xi": CT(r1.results[c]["xz"][0:DI]), "tailp": tailp,
            "cw": cw_pm, "cb": pm(p["L1_attn_b_conv_b"]),
            "xpw": CT(p["L1_attn_b_x_proj_w"].T),
            "dtw": CT(p["L1_attn_b_dt_proj_w"].T),
            "dtb": pm(p["L1_attn_b_dt_proj_b"]),
        })
    r2 = _run("mpre2", in2)
    in3 = []
    for c in range(NCORES):
        b, ds = c // 4, c % 4
        dt_full = np.concatenate([r2.results[4 * b + j]["dto"] for j in range(4)], axis=1)
        xc_full = np.concatenate([r2.results[4 * b + j]["xc"] for j in range(4)], axis=1)
        bc_full = np.concatenate([r2.results[4 * b + j]["bco"] for j in range(4)], axis=1)
        dsl = slice(ds * 512, (ds + 1) * 512)
        Al = CT(p["L1_attn_b_A_log"][dsl].reshape(64, 8, 16).reshape(64, 128).T)
        in3.append({"dt": CT(dt_full[dsl]), "xcs": CT(xc_full[dsl]),
                    "Bv": CT(bc_full[0:16]), "Cv": CT(bc_full[16:32]),
                    "Al": Al, "blk8": blk8_mat()})
    r3 = _run("mscan", in3)
    in4 = []
    for c in range(NCORES):
        b, qq = c // 4, c % 4
        y_full = np.concatenate([r3.results[4 * b + j]["yo"] for j in range(4)], axis=0)
        tsl = slice(qq * TC, (qq + 1) * TC)
        in4.append({
            "x": xs[c], "y": CT(y_full[:, tsl]), "xcin": r2.results[c]["xc"],
            "zf": CT(r1.results[c]["xz"][DI:2 * DI]),
            "opw": CT(p["L1_attn_b_out_proj_w"].T),
            "Dv": pm(p["L1_attn_b_D"]), "lng": pm(p["L1_attn_b_ln_g"]),
            "lnb": pm(p["L1_attn_b_ln_b"]),
        })
    r = _run("mpost", in4)
    xs = [r.results[c]["xo"] for c in range(NCORES)]

    r = _run("moe", [{**mo, "x": xs[c]} for c in range(NCORES)])
    return outs_to_full(r)
